# version 07
# baseline (speedup 1.0000x reference)
import numpy as np
import concourse.bass as bass
import concourse.mybir as mybir
from concourse.bass_utils import run_bass_kernel_spmd

F32 = mybir.dt.float32
BF16 = mybir.dt.bfloat16
I32 = mybir.dt.int32
U8 = mybir.dt.uint8
AF = mybir.ActivationFunctionType
ALU = mybir.AluOpType
AX = mybir.AxisListType

B = 2
L = 8192
D = 1024
DI = 2048
DH = 1024
N16 = 16
DTR = 64
HD = 128
E8 = 8
TOK = 2048
HT = 1024
TS = 512
NSUB_F = 8
NSUB_B = 16
NG = 2
EPS = 1e-5
SCATT = float(1.0 / np.sqrt(128.0))
NEG = -1e30

ENGS = ("sp", "v", "s", "g", "t")
CONSTS = [1e-5, -0.5, -1.0, 1.0, 1.0 / 1024.0, float(1.0 / np.sqrt(128.0)), 0.0]
FL_DBC, FL_Y, FL_KV = 1, 2, 3
DEBUG = False
LAST_RES = [None]


def kernel(**inp):
    x = np.ascontiguousarray(np.asarray(inp["L0_attn_x"], dtype=np.float32))
    W_in = np.asarray(inp["L0_attn_W_in"], np.float32)
    conv_w = np.asarray(inp["L0_attn_conv_w"], np.float32)
    conv_b = np.asarray(inp["L0_attn_conv_b"], np.float32)
    W_xp = np.asarray(inp["L0_attn_W_xproj"], np.float32)
    W_dt = np.asarray(inp["L0_attn_W_dt"], np.float32)
    b_dt = np.asarray(inp["L0_attn_b_dt"], np.float32)
    A_log = np.asarray(inp["L0_attn_A_log"], np.float32)
    Dv = np.asarray(inp["L0_attn_D"], np.float32)
    W_out = np.asarray(inp["L0_attn_W_out"], np.float32)

    def tr(a, nt):
        m = a.shape[1]
        return np.ascontiguousarray(
            a.reshape(nt, 128, m).transpose(1, 0, 2).reshape(128, nt * m))

    def col8(a):
        return np.ascontiguousarray(a.reshape(8, 128).T)

    in_maps = []
    for c in range(8):
        b, p = c // 4, c % 4
        q = p % 2
        dlo, dhi = q * DH, (q + 1) * DH
        in_maps.append({
            "x_full": x[b],
            "wiu": tr(W_in[:, dlo:dhi], 8),
            "wiz": tr(W_in[:, DI + dlo:DI + dhi], 8),
            "convw": np.ascontiguousarray(
                conv_w[dlo:dhi, 0, :].reshape(8, 128, 4).transpose(1, 0, 2).reshape(128, 32)),
            "convb": col8(conv_b[dlo:dhi]),
            "wxp": tr(W_xp[dlo:dhi], 8),
            "wdt": np.concatenate([np.asarray(W_dt[:, dlo:dhi], np.float32),
                                   np.zeros((128 - DTR, DH), np.float32)], axis=0),
            "bdt": col8(b_dt[dlo:dhi]),
            "alog": tr(A_log[dlo:dhi], 8),
            "dvec": col8(Dv[dlo:dhi]),
            "wout": tr(W_out, 16),
            "lnw": np.asarray(inp["L0_attn_ln_w"], np.float32)[None, :],
            "lnb": np.asarray(inp["L0_attn_ln_b"], np.float32)[None, :],
            "ln1w": np.asarray(inp["L0_ffn_ln1_w"], np.float32)[None, :],
            "ln1b": np.asarray(inp["L0_ffn_ln1_b"], np.float32)[None, :],
            "ln2w": np.asarray(inp["L0_ffn_ln2_w"], np.float32)[None, :],
            "ln2b": np.asarray(inp["L0_ffn_ln2_b"], np.float32)[None, :],
            "ssim": np.asarray(inp["L0_ffn_smha_sim"], np.float32),
            "sgate": np.asarray(inp["L0_ffn_smha_gates"], np.float32)[None, :],
            "msim": np.asarray(inp["L0_ffn_moe_sim"], np.float32),
            "mgate": np.asarray(inp["L0_ffn_moe_gates"], np.float32)[None, :],
            "qf": tr(np.asarray(inp["L0_ffn_q_proj"], np.float32).reshape(E8 * D, HD), 64),
            "kf": tr(np.asarray(inp["L0_ffn_k_proj"], np.float32).reshape(E8 * D, HD), 64),
            "vf": tr(np.asarray(inp["L0_ffn_v_proj"], np.float32).reshape(E8 * D, HD), 64),
            "of": tr(np.asarray(inp["L0_ffn_o_proj"], np.float32).reshape(E8 * HD, D), 8),
            "w1": np.asarray(inp["L0_ffn_w1"], np.float32),
            "w2": np.asarray(inp["L0_ffn_w2"], np.float32),
        })

    nc = build_program()
    res = run_bass_kernel_spmd(nc, in_maps, core_ids=list(range(8)))
    LAST_RES[0] = res
    out = np.empty((B, L, D), dtype=np.float32)
    for c in range(8):
        b, p = c // 4, c % 4
        out[b, p * TOK:(p + 1) * TOK] = res.results[c]["out_x2"]
    return out


class Bld:
    def __init__(self, nc, es):
        self.nc = nc
        self.es = es
        self.ops = {e: [] for e in ENGS}
        self.nreg = 0
        self.rt = {}
        self.E = {e: self.es.enter_context(nc.semaphore("E_" + e)) for e in ENGS}
        self.pollsem = self.es.enter_context(nc.semaphore("E_poll"))
        self.cnt = {e: 0 for e in ENGS}

    def rtag(self):
        self.nreg += 1
        return "r%d" % self.nreg

    def do(self, eng, fn):
        self.ops[eng].append(fn)

    def op(self, eng, fn):
        self.cnt[eng] += 1
        h = self.E[eng]
        self.ops[eng].append(lambda e: fn(e).then_inc(h, 1))
        return self.cnt[eng]

    def dma(self, eng, fn):
        self.cnt[eng] += 16
        h = self.E[eng]
        self.ops[eng].append(lambda e: fn(e).then_inc(h, 16))
        return self.cnt[eng]

    def mark(self, eng):
        return self.cnt[eng]

    def spacer(self, eng):
        j = self.junk
        if eng == "s":
            return self.op(eng, lambda e: e.activation(out=j, in_=j, func=AF.Identity))
        return self.op(eng, lambda e: e.tensor_copy(out=j, in_=j))

    def tiny(self, eng, fn):
        self.op(eng, fn)
        return self.spacer(eng)

    def wait(self, waiter, producer, val=None):
        v = self.cnt[producer] if val is None else val
        if v <= 0:
            return
        h = self.E[producer]
        self.ops[waiter].append(lambda e: e.wait_ge(h, v))

    def barrier(self):
        for e in ENGS:
            self.op(e, lambda en: en.nop())
        marks = {e: self.cnt[e] for e in ENGS}
        for w in ENGS:
            for pd in ENGS:
                if w != pd:
                    self.wait(w, pd, marks[pd])

    def open_pid_reg(self, eng):
        rt = self.rt

        def go(e):
            cm = e.register("preg_" + eng)
            r = cm.__enter__()
            pid = e.partition_id()
            e.reg_alu(r, pid, 3, ALU.bitwise_and)
            rt[("preg", eng)] = (r, cm)
            if eng == "sp":
                cm2 = e.register("pollcnt")
                r2 = cm2.__enter__()
                e.reg_mov(r2, 16)
                rt[("pollcnt",)] = (r2, cm2)
        self.ops[eng].append(go)

    def open_if_p(self, eng, pval):
        rt = self.rt

        def go(e):
            gd = e.If_eq(rt[("preg", eng)][0], pval)
            gd.__enter__()
            rt[("guard", eng)] = gd
        self.ops[eng].append(go)

    def close_if(self, eng):
        rt = self.rt

        def go(e):
            rt[("guard", eng)].__exit__(None, None, None)
        self.ops[eng].append(go)

    def poll(self, eng, ftile, fslot, fsrc, want):
        assert eng == "sp"
        tag = self.rtag()
        ph = self.pollsem
        h = self.E[eng]
        rt = self.rt
        self.cnt[eng] += 1

        def go(e):
            it = rt[("pollcnt",)][0]
            with e.register("pf" + tag) as fv, e.register("pc" + tag) as cnd:
                e.reg_mov(fv, -1)
                e.reg_alu(cnd, fv, want, ALU.is_lt)
                with e.While(cnd):
                    e.dma_start(out=ftile[0:1, fslot:fslot + 1], in_=fsrc).then_inc(ph, 16)
                    e.reg_add(it, it, 16)
                    e.wait_ge(ph, it)
                    e.reg_load(fv, ftile[0:1, fslot:fslot + 1])
                    e.reg_alu(cnd, fv, want, ALU.is_lt)
            return e.sem_inc(h, 1)
        self.ops[eng].append(go)


class Arena:

    def __init__(self, base_ap, words):
        self.base = base_ap
        self.words = words

    def f32(self, off, cols, parts=128):
        assert off + cols <= self.words, (off, cols)
        return self.base[0:parts, off:off + cols]

    def bf16(self, off, cols, parts=128):
        assert off + (cols + 1) // 2 <= self.words, (off, cols)
        return self.base.bitcast(BF16)[0:parts, 2 * off:2 * off + cols]

    def i32(self, off, cols, parts=128):
        assert off + cols <= self.words, (off, cols)
        return self.base.bitcast(I32)[0:parts, off:off + cols]


def build_program():
    nc = bass.Bass("TRN2", target_bir_lowering=False)
    from contextlib import ExitStack
    es = ExitStack()
    bld = Bld(nc, es)

    d = {}
    for nm, sh in [
        ("x_full", [L, D]), ("wiu", [128, 8192]), ("wiz", [128, 8192]),
        ("convw", [128, 32]), ("convb", [128, 8]), ("wxp", [128, 8 * 96]),
        ("wdt", [128, DH]), ("bdt", [128, 8]), ("alog", [128, 128]),
        ("dvec", [128, 8]), ("wout", [128, 16384]),
        ("lnw", [1, D]), ("lnb", [1, D]), ("ln1w", [1, D]), ("ln1b", [1, D]),
        ("ln2w", [1, D]), ("ln2b", [1, D]),
        ("ssim", [D, E8]), ("sgate", [1, E8]), ("msim", [D, E8]), ("mgate", [1, E8]),
        ("qf", [128, 8192]), ("kf", [128, 8192]), ("vf", [128, 8192]),
        ("of", [128, 8192]), ("w1", [E8, D, 1024]), ("w2", [E8, 1024, D]),
    ]:
        d[nm] = nc.dram_tensor(nm, sh, F32, kind="ExternalInput").ap()
    d["out_x2"] = nc.dram_tensor("out_x2", [TOK, D], F32, kind="ExternalOutput").ap()

    dbgk = dict(kind="ExternalOutput") if DEBUG else {}
    d["zsp"] = nc.dram_tensor("zsp", [NSUB_B, DH, TS], F32, **dbgk).ap()
    d["x1own"] = nc.dram_tensor("x1own", [TOK, D], F32, **dbgk).ap()
    d["x1kv"] = nc.dram_tensor("x1kv", [TOK, D], F32, **dbgk).ap()
    d["xpp"] = nc.dram_tensor("xpp", [TOK, D], F32, **dbgk).ap()
    d["bcd"] = nc.dram_tensor("bcd", [32, TS], F32).ap()
    d["rwTd"] = nc.dram_tensor("rwTd", [E8, TOK], F32,
                               **(dict(kind="ExternalOutput") if DEBUG else {})).ap()
    d["g8"] = nc.dram_tensor("g8", [16], F32).ap()

    if DEBUG:
        d["dbg_ntT"] = nc.dram_tensor("dbg_ntT", [NSUB_B, 128, 4096], BF16,
                                      kind="ExternalOutput").ap()
        d["dbg_usil"] = nc.dram_tensor("dbg_usil", [NSUB_B, 128, 4096], F32,
                                       kind="ExternalOutput").ap()
        d["dbg_dbc"] = nc.dram_tensor("dbg_dbc", [NSUB_B, 96, TS], F32,
                                      kind="ExternalOutput").ap()
        d["dbg_ypub"] = nc.dram_tensor("dbg_ypub", [NSUB_B, 128, 4096], BF16,
                                       kind="ExternalOutput").ap()
        d["dbg_delt"] = nc.dram_tensor("dbg_delt", [NSUB_B, 128, TS], F32,
                                       kind="ExternalOutput").ap()
    if DEBUG:
        d["dbg_scan"] = nc.dram_tensor("dbg_scan", [6, 128, NG * TS], F32,
                                       kind="ExternalOutput").ap()
        d["dbg_amat"] = nc.dram_tensor("dbg_amat", [128, 144], F32,
                                       kind="ExternalOutput").ap()
        d["dbg_yac"] = nc.dram_tensor("dbg_yac", [8, 128, TS], F32,
                                      kind="ExternalOutput").ap()
        d["dbg_kT"] = nc.dram_tensor("dbg_kT", [2, 128, 1024], BF16,
                                     kind="ExternalOutput").ap()
        d["dbg_vsb"] = nc.dram_tensor("dbg_vsb", [2, 128, 1024], BF16,
                                      kind="ExternalOutput").ap()
        d["dbg_qT"] = nc.dram_tensor("dbg_qT", [2, 128, 1024], BF16,
                                     kind="ExternalOutput").ap()
        d["dbg_hs"] = nc.dram_tensor("dbg_hs", [16, 128, 1024], F32,
                                     kind="ExternalOutput").ap()
        d["dbg_hs2"] = nc.dram_tensor("dbg_hs2", [16, 128, 1024], F32,
                                      kind="ExternalOutput").ap()
        d["dbg_moe"] = nc.dram_tensor("dbg_moe", [8, 128, 2048], F32,
                                      kind="ExternalOutput").ap()
    d["shm_dbc"] = nc.dram_tensor("shm_dbc", [2, NSUB_B, 96, TS], F32,
                                  kind="Internal", addr_space="Shared").ap()
    d["shm_y"] = nc.dram_tensor("shm_y", [2, NSUB_B, 128, 8 * TS], BF16,
                                kind="Internal", addr_space="Shared").ap()
    d["shm_k"] = nc.dram_tensor("shm_k", [4, 128, TOK], BF16,
                                kind="Internal", addr_space="Shared").ap()
    d["shm_v"] = nc.dram_tensor("shm_v", [4, 128, 16 * HD], BF16,
                                kind="Internal", addr_space="Shared").ap()
    d["shm_flg"] = nc.dram_tensor("shm_flg", [2, 64], I32,
                                  kind="Internal", addr_space="Shared").ap()

    AW = 48000
    with es:
        arena_t = es.enter_context(nc.sbuf_tensor("arena", [128, AW], F32))
        A = Arena(arena_t[:], AW)
        t, PS = layout(A, nc, es)
        bld.junk = t["junk"]
        for i, cv in enumerate(CONSTS):
            nc.const_aps.register(F32, cv, t["cns"][:, i:i + 1])
        ctx = dict(t=t, P=PS, d=d, A=A)
        boot(bld, ctx)
        snap = dict(bld.cnt)
        for eng in ENGS:
            bld.open_pid_reg(eng)
        for p in range(4):
            for eng in ENGS:
                bld.open_if_p(eng, p)
            bld.cnt = dict(snap)
            klass(bld, ctx, p)
            for eng in ENGS:
                bld.close_if(eng)

        with nc.Block() as block:
            @block.sync
            def _(e):
                for fn in bld.ops["sp"]:
                    fn(e)

            @block.vector
            def _(e):
                for fn in bld.ops["v"]:
                    fn(e)

            @block.scalar
            def _(e):
                for fn in bld.ops["s"]:
                    fn(e)

            @block.gpsimd
            def _(e):
                for fn in bld.ops["g"]:
                    fn(e)

            @block.tensor
            def _(e):
                for fn in bld.ops["t"]:
                    fn(e)
    return nc


def layout(A, nc, es):
    t = {}
    o = 0

    def take(words):
        nonlocal o
        r = o
        o += words
        return r

    t["amat"] = A.f32(take(128), 128)
    t["convw"] = A.f32(take(32), 32)
    t["convb"] = A.f32(take(8), 8)
    t["bdt"] = A.f32(take(8), 8)
    t["dvec"] = A.f32(take(8), 8)
    t["wdt"] = A.bf16(take(512), 1024, parts=64)
    t["wxp"] = A.bf16(take(384), 768)
    t["lnw"] = A.f32(take(1024), 1024)
    t["lnb"] = A.f32(take(1024), 1024)
    t["ident"] = A.bf16(take(64), 128)
    t["identf"] = A.f32(take(128), 128)
    t["ci"] = A.i32(take(64), 64, parts=1)
    t["zi"] = A.i32(take(64), 64, parts=1)
    t["ftile"] = A.i32(take(16), 16, parts=1)
    t["neg8"] = A.f32(take(8), 8)
    t["stat"] = A.f32(take(16), 16)
    t["cns"] = A.f32(take(8), 8)
    t["junk"] = A.f32(take(256), 256)
    t["sq"] = A.f32(take(1024), 1024)
    t["x1a"] = A.f32(take(1024), 1024)
    t["hcar"] = A.f32(take(128), 128)
    pers_end = o

    o = pers_end
    t["wiu"] = A.bf16(take(4096), 8192)
    t["wiz"] = A.bf16(take(4096), 8192)
    t["wout"] = A.bf16(take(8192), 16384)
    t["xr0"] = A.f32(take(1024), 1024)
    t["xr1"] = A.f32(take(1024), 1024)
    t["ln0r0"] = A.f32(take(1024), 1024)
    t["ln0r1"] = A.f32(take(1024), 1024)
    t["ntb"] = A.bf16(take(512), 1024)
    t["ntT"] = A.bf16(take(2048), 4096)
    t["urx"] = A.f32(take(2 * (TS + 4)), 2 * (TS + 4))
    t["utail"] = A.f32(take(32), 32)
    t["usil"] = A.f32(take(4096), 4096)
    zq_off = take(512)
    t["zq"] = A.f32(zq_off, 512)
    t["sp2"] = A.f32(zq_off, 512)
    t["dbc1"] = A.f32(take(512), 512, parts=96)
    delt_off = take(512)
    t["delt"] = A.f32(delt_off, 512)
    t["dbc2"] = A.f32(delt_off, 512, parts=96)
    t["dtbf"] = A.bf16(take(256), 512, parts=64)
    t["du"] = A.f32(take(512), 512)
    t["dA"] = A.f32(take(NG * 512), NG * 512)
    t["wv"] = A.f32(take(NG * 512), NG * 512)
    t["hb"] = A.f32(take(NG * 512), NG * 512)
    t["bbc"] = A.f32(take(NG * 512), NG * 512)
    t["cbc"] = A.f32(take(NG * 512), NG * 512)
    t["yac"] = A.f32(take(512), 512)
    t["yfin"] = A.f32(take(512), 512)
    t["ypub"] = A.bf16(take(2048), 4096)
    t["ymate"] = A.bf16(take(2048), 4096)
    t["zrl"] = A.f32(take(512), 512)
    assert o <= A.words, ("mamba", o)

    t["stg"] = A.f32(pers_end + 16384, 8192)

    o = pers_end
    t["hs"] = A.f32(take(8 * 1024), 8192)
    t["nT"] = A.bf16(take(4096), 8192)
    pjw_off = take(4096)
    t["pjw"] = A.bf16(pjw_off, 8192)
    t["fstgA"] = A.f32(pjw_off, 4096)
    t["qTa"] = A.bf16(take(512), 1024)
    t["qTb"] = A.bf16(take(512), 1024)
    t["kT"] = A.bf16(take(512), 1024)
    t["vsb"] = A.bf16(take(512), 1024)
    kall_off = take(4096)
    t["kall"] = A.bf16(kall_off, 8192)
    vall_off = take(4096)
    t["vall"] = A.bf16(vall_off, 8192)
    assert vall_off == kall_off + 4096
    t["fstgB"] = A.f32(kall_off, 8192)
    t["w1c"] = t["kall"]
    t["w2c"] = t["vall"]
    t["Sb"] = A.f32(take(2048), 2048)
    t["Pb"] = A.bf16(take(1024), 2048)
    t["n1f"] = A.f32(take(1024), 1024)
    t["rwTs"] = A.f32(take(2048), 2048, parts=8)
    t["rwbc"] = A.f32(take(512), 512)
    t["n1wt"] = A.bf16(take(256), 512)
    t["PTs"] = A.bf16(take(64), 128)
    t["osb"] = A.f32(take(128), 128)
    t["otb"] = A.f32(take(128), 128)
    t["awt"] = A.bf16(take(512), 1024)
    t["gel"] = A.f32(take(256), 256)
    t["iwb"] = A.bf16(take(1024), 2048)
    t["simb"] = A.bf16(take(32), 64)
    t["simi"] = A.f32(take(8), 8)
    t["sgb"] = A.f32(take(8), 8)
    t["n1ni"] = A.f32(take(8), 8)
    t["rw8"] = A.f32(take(8), 8)
    t["lg8"] = A.f32(take(8), 8)
    t["mk8"] = A.f32(take(8), 8)
    t["fb8"] = A.f32(take(8), 8)
    t["m8"] = A.f32(take(8), 8)
    t["scr8"] = A.f32(take(8), 8)
    msk_off = take(8)
    t["msk8"] = A.base.bitcast(U8)[0:128, 4 * msk_off:4 * msk_off + 8]
    t["simsq"] = A.f32(take(64), 64)
    t["cmask"] = A.f32(take(2048), 2048)
    assert o <= A.words, ("ffn", o)

    PS = {}

    def psum(name, shape, dtype=F32):
        return es.enter_context(nc.psum_tensor(name, shape, dtype))

    PS["A"] = psum("pA", [128, TS])
    PS["B2"] = psum("pB2", [128, TS])
    PS["C"] = psum("pC", [128, TS])
    PS["D2"] = psum("pD2", [128, TS])
    PS["E2"] = psum("pE2", [128, TS])
    PS["F2"] = psum("pF2", [128, TS])
    PS["T"] = psum("pT", [128, 512], BF16)
    PS["O"] = psum("pO", [128, 128])
    return t, PS


def boot(bld, ctx):
    t, P, d = ctx["t"], ctx["P"], ctx["d"]
    stage = t["stg"]

    def mkconst_a(e):
        e.memset(t["ident"], 1.0)
        e.memset(t["identf"], 1.0)
        e.memset(t["zi"], 0)
        e.memset(t["hcar"], 0.0)
        e.memset(t["neg8"], NEG)
        for i, cv in enumerate(CONSTS):
            e.memset(t["cns"][:, i:i + 1], cv)
        return e.iota(t["ci"], pattern=[[1, 64]], base=1, channel_multiplier=0)

    def mkconst_b(e):
        e.affine_select(out=t["ident"], in_=t["ident"], pattern=[[-1, 128]],
                        compare_op=ALU.is_equal, fill=0.0, base=0, channel_multiplier=1)
        return e.affine_select(out=t["identf"], in_=t["identf"], pattern=[[-1, 128]],
                               compare_op=ALU.is_equal, fill=0.0, base=0,
                               channel_multiplier=1)
    bld.op("g", mkconst_a)
    bld.wait("g", "g")
    mg0 = bld.op("g", mkconst_b)
    loads = [
        (t["wiu"][:, 0:8192], d["wiu"], 8192),
        (t["wiz"][:, 0:8192], d["wiz"], 8192),
        (t["wout"][:, 0:8192], d["wout"][:, 0:8192], 8192),
        (t["wout"][:, 8192:16384], d["wout"][:, 8192:16384], 8192),
        (t["wxp"][:, 0:768], d["wxp"], 768),
    ]
    for (dst, src, colsn) in loads:
        bld.wait("sp", "v")
        m = bld.dma("sp", lambda e, src=src, c=colsn: e.dma_start(
            out=stage[:, 0:c], in_=src))
        bld.wait("v", "sp", m)
        bld.op("v", lambda e, dst=dst, c=colsn: e.tensor_copy(
            out=dst, in_=stage[:, 0:c]))
    bld.wait("sp", "v")
    m = bld.dma("sp", lambda e: e.dma_start(out=stage[:, 0:DH], in_=d["wdt"]))
    bld.wait("sp", "sp")
    bld.wait("v", "sp", m)
    bld.op("v", lambda e: e.tensor_copy(out=t["wdt"], in_=stage[0:DTR, 0:DH]))
    bld.wait("g", "v")
    bld.wait("g", "sp")
    bld.op("g", lambda e: e.memset(t["urx"], 0.0))
    bld.op("g", lambda e: e.memset(t["utail"], 0.0))
    for (dst, src) in [(t["convw"], d["convw"]), (t["convb"], d["convb"]),
                       (t["bdt"], d["bdt"]), (t["dvec"], d["dvec"]),
                       (t["amat"], d["alog"])]:
        bld.dma("sp", lambda e, dd=dst, s2=src: e.dma_start(out=dd, in_=s2))
    for nm, src in (("lnw", "lnw"), ("lnb", "lnb")):
        bld.dma("sp", lambda e, nm=nm, src=src: e.dma_start(
            out=t[nm], in_=d[src].partition_broadcast(128)))
    bld.wait("s", "sp")
    bld.wait("s", "g", mg0)
    bld.op("s", lambda e: e.activation(out=t["amat"], in_=t["amat"], func=AF.Exp))
    bld.op("s", lambda e: e.mul(out=t["amat"], in_=t["amat"], mul=-1.0))


    zsem = bld.pollsem

    def zfl(e):
        tag = bld.rtag()
        pid = e.partition_id()
        with e.register("zq" + tag) as qr:
            e.reg_alu(qr, pid, 1, ALU.bitwise_and)
            with e.If_eq(qr, 0):
                e.dma_start(out=d["shm_flg"][0:1, :], in_=t["zi"]).then_inc(zsem, 16)
            with e.Else():
                e.dma_start(out=d["shm_flg"][1:2, :], in_=t["zi"]).then_inc(zsem, 16)
        return e.wait_ge(zsem, 16)
    bld.wait("sp", "g")
    bld.op("sp", zfl)
    bld.barrier()


def klass(bld, ctx, p):
    t, P, d = ctx["t"], ctx["P"], ctx["d"]
    q = p % 2
    back = p >= 2
    NS = NSUB_B if back else NSUB_F
    myrow, materow = q, 1 - q

    def ln_1024(src_ap, dst_ap, affine, accum_to=None):
        nmu = t["stat"][:, 0:1]
        var = t["stat"][:, 1:2]
        rstd = t["stat"][:, 2:3]
        bld.tiny("v", lambda e: e.tensor_reduce(
            out=nmu, in_=src_ap, axis=AX.X, op=ALU.add, negate=True))
        bld.tiny("v", lambda e: e.tensor_scalar_mul(out=nmu, in0=nmu, scalar1=1.0 / D))
        bld.wait("s", "v")
        bld.tiny("s", lambda e: e.activation(
            out=t["sq"], in_=src_ap, func=AF.Square, bias=nmu, accum_out=var))
        bld.tiny("s", lambda e: e.activation(
            out=rstd, in_=var, func=AF.Ln, scale=1.0 / D, bias=EPS))
        ms = bld.tiny("s", lambda e: e.activation(out=rstd, in_=rstd, func=AF.Exp, scale=-0.5))
        bld.wait("v", "s", ms)
        if affine is None:
            mv = bld.op("v", lambda e: e.tensor_scalar(
                out=dst_ap, in0=src_ap, scalar1=nmu, scalar2=rstd,
                op0=ALU.add, op1=ALU.mult))
        else:
            wt, bt = affine
            bld.op("v", lambda e: e.tensor_scalar(
                out=t["sq"], in0=src_ap, scalar1=nmu, scalar2=rstd,
                op0=ALU.add, op1=ALU.mult))
            bld.op("v", lambda e: e.tensor_mul(out=t["sq"], in0=t["sq"], in1=wt))
            mv = bld.op("v", lambda e: e.tensor_add(out=dst_ap, in0=t["sq"], in1=bt))
        if accum_to is not None:
            bld.wait("s", "v", mv)
            bld.op("s", lambda e: e.activation(
                out=t["sq"], in_=dst_ap, func=AF.Square, accum_out=accum_to))
        return mv

    def poll_mate(slot, want):
        bld.poll("sp", t["ftile"], slot,
                 d["shm_flg"][materow:materow + 1, slot:slot + 1], want)

    def flag_self(slot, val):
        bld.wait("sp", "sp")
        bld.dma("sp", lambda e: e.dma_start(
            out=d["shm_flg"][myrow:myrow + 1, slot:slot + 1],
            in_=t["ci"][0:1, val - 1:val]))

    H = dict(ln_1024=ln_1024, poll_mate=poll_mate, flag_self=flag_self)

    for s in range(NS):
        mamba_subtile(bld, ctx, p, s, H)

    bld.barrier()
    bld.dma("sp", lambda e: e.dma_start(out=t["lnw"], in_=d["ln1w"].partition_broadcast(128)))
    mlp1 = bld.dma("sp", lambda e: e.dma_start(out=t["lnb"], in_=d["ln1b"].partition_broadcast(128)))
    bld.wait("v", "sp", mlp1)
    bld.wait("s", "sp", mlp1)

    kvcnt = [0]
    if back:
        for h in range(2):
            ffn_stage1(bld, ctx, p, H, x1src=d["x1kv"], half=h, chunk=q,
                       kv_only=True, kvcnt=kvcnt)
    for h in range(2):
        ffn_stage1(bld, ctx, p, H, x1src=d["x1own"], half=h, chunk=p,
                   kv_only=False, kvcnt=kvcnt)
    bld.barrier()
    ffn_stage2(bld, ctx, p, H, kvcnt)

    for w in ENGS:
        bld.wait(w, "sp")


def mamba_subtile(bld, ctx, p, s, H):
    t, P, d = ctx["t"], ctx["P"], ctx["d"]
    q = p % 2
    back = p >= 2
    own = (p * 4) <= s < (p * 4 + 4)
    kv = back and (q * 4) <= s < (q * 4 + 4)
    fin = own or kv
    myrow, materow = q, 1 - q
    ln_1024, poll_mate, flag_self = H["ln_1024"], H["poll_mate"], H["flag_self"]

    for tt in range(4):
        xr = t["xr0"] if tt % 2 == 0 else t["xr1"]
        ln0 = t["ln0r0"] if tt % 2 == 0 else t["ln0r1"]
        bld.wait("sp", "v")
        bld.wait("sp", "s")
        mx = bld.dma("sp", lambda e, s=s, tt=tt, xr=xr: e.dma_start(
            out=xr, in_=d["x_full"][s * TS + tt * 128:s * TS + (tt + 1) * 128, :]))
        bld.wait("v", "sp", mx)
        ln_1024(xr, ln0, None)
        ln_1024(ln0, t["ntb"], (t["lnw"], t["lnb"]))
        bld.wait("t", "v")
        for cb in range(8):
            bld.op("t", lambda e, cb=cb: e.transpose(
                out=P["T"][:, (cb % 4) * 128:(cb % 4) * 128 + 128],
                in_=t["ntb"][:, cb * 128:(cb + 1) * 128], identity=t["ident"]))
            if cb % 4 == 3:
                bld.wait("v", "t")
                g0 = cb - 3
                bld.op("v", lambda e, g0=g0, tt=tt: e.tensor_copy(
                    out=_ntT_dst(t, g0, tt), in_=P["T"][:, 0:512]))
                bld.wait("t", "v")

    for dt in range(8):
        pm = P["A"] if dt % 2 == 0 else P["B2"]
        bld.wait("t", "v")
        bld.wait("t", "s")
        for cb in range(8):
            bld.op("t", lambda e, cb=cb, dt=dt, pm=pm: e.matmul(
                out=pm[:], lhsT=t["wiu"][:, cb * 1024 + dt * 128:cb * 1024 + dt * 128 + 128],
                rhs=_ntT_mov(t, cb), start=(cb == 0), stop=(cb == 7)))
        seg = t["urx"][:, (dt % 2) * (TS + 4):(dt % 2 + 1) * (TS + 4)]
        bld.wait("v", "t")
        bld.op("v", lambda e, pm=pm, seg=seg: e.tensor_copy(out=seg[:, 4:4 + TS], in_=pm[:]))
        bld.wait("t", "v")
        bld.op("v", lambda e, seg=seg, dt=dt: e.tensor_copy(
            out=seg[:, 1:4], in_=t["utail"][:, dt * 4:dt * 4 + 3]))
        acc = t["du"]
        bld.wait("v", "s")
        bld.op("v", lambda e, seg=seg, dt=dt, acc=acc: e.tensor_scalar(
            out=acc, in0=seg[:, 4:4 + TS], scalar1=t["convw"][:, dt * 4 + 3:dt * 4 + 4],
            scalar2=None, op0=ALU.mult))
        for k in range(3):
            bld.op("v", lambda e, seg=seg, dt=dt, k=k, acc=acc: e.scalar_tensor_tensor(
                out=acc, in0=seg[:, 1 + k:1 + k + TS],
                scalar=t["convw"][:, dt * 4 + k:dt * 4 + k + 1], in1=acc,
                op0=ALU.mult, op1=ALU.add))
        bld.op("v", lambda e, seg=seg, dt=dt: e.tensor_copy(
            out=t["utail"][:, dt * 4:dt * 4 + 3], in_=seg[:, 1 + TS:4 + TS]))
        bld.wait("s", "v")
        bld.op("s", lambda e, dt=dt, acc=acc: e.activation(
            out=t["usil"][:, dt * TS:(dt + 1) * TS], in_=acc, func=AF.Silu,
            bias=t["convb"][:, dt:dt + 1]))
        pm2 = P["C"] if dt % 2 == 0 else P["D2"]
        bld.wait("t", "s")
        for cb in range(8):
            bld.op("t", lambda e, cb=cb, dt=dt, pm2=pm2: e.matmul(
                out=pm2[:], lhsT=t["wiz"][:, cb * 1024 + dt * 128:cb * 1024 + dt * 128 + 128],
                rhs=_ntT_mov(t, cb), start=(cb == 0), stop=(cb == 7)))
        bld.wait("s", "t")
        bld.wait("s", "sp")
        bld.op("s", lambda e, pm2=pm2: e.activation(out=t["zq"], in_=pm2[:], func=AF.Silu))
        bld.wait("sp", "s")
        bld.dma("sp", lambda e, s=s, dt=dt: e.dma_start(
            out=d["zsp"][s, dt * 128:(dt + 1) * 128, :], in_=t["zq"]))

    for dt in range(8):
        bld.wait("v", "s")
        bld.wait("v", "t")
        bld.op("v", lambda e, dt=dt: e.tensor_copy(
            out=_ubf(t), in_=t["usil"][:, dt * TS:(dt + 1) * TS]))
        bld.wait("t", "v")
        bld.op("t", lambda e, dt=dt: e.matmul(
            out=P["E2"][0:96, :], lhsT=t["wxp"][:, dt * 96:(dt + 1) * 96],
            rhs=_ubf(t), start=(dt == 0), stop=(dt == 7)))
    bld.wait("v", "t")
    bld.op("v", lambda e: e.tensor_copy(out=t["dbc1"], in_=P["E2"][0:96, :]))
    bld.wait("t", "v")
    bld.wait("sp", "v")
    bld.dma("sp", lambda e, s=s: e.dma_start(out=d["shm_dbc"][myrow, s], in_=t["dbc1"]))
    flag_self(FL_DBC, s + 1)
    poll_mate(FL_DBC, s + 1)
    mdbc = bld.dma("sp", lambda e, s=s: e.dma_start(
        out=t["dbc2"], in_=d["shm_dbc"][materow, s]))
    bld.wait("v", "sp", mdbc)
    bld.op("v", lambda e: e.tensor_add(out=t["dbc1"], in0=t["dbc1"], in1=t["dbc2"]))
    mtb = bld.op("v", lambda e: e.tensor_copy(out=t["dtbf"], in_=t["dbc1"][0:64, :]))
    bld.wait("sp", "v", mtb)
    bld.dma("sp", lambda e: e.dma_start(out=d["bcd"][:], in_=t["dbc1"][64:96, :]))

    for dt in range(8):
        bld.wait("t", "v")
        bld.op("t", lambda e, dt=dt: e.matmul(
            out=P["F2"][:], lhsT=t["wdt"][:, dt * 128:(dt + 1) * 128],
            rhs=t["dtbf"], start=True, stop=True))
        bb = t["bdt"][:, dt:dt + 1]
        bld.wait("s", "t")
        bld.wait("s", "v")
        bld.op("s", lambda e, bb=bb: e.activation(
            out=t["sp2"], in_=P["F2"][:], func=AF.Abs, bias=bb))
        bld.op("s", lambda e: e.activation(out=t["sp2"], in_=t["sp2"], func=AF.Exp, scale=-1.0))
        bld.op("s", lambda e: e.activation(out=t["sp2"], in_=t["sp2"], func=AF.Ln, bias=1.0))
        ms = bld.op("s", lambda e, bb=bb: e.activation(
            out=t["delt"], in_=P["F2"][:], func=AF.Relu, bias=bb))
        bld.wait("v", "s", ms)
        bld.op("v", lambda e: e.tensor_add(out=t["delt"], in0=t["delt"], in1=t["sp2"]))
        bld.wait("t", "v")
        mdu = bld.op("v", lambda e, dt=dt: e.tensor_mul(
            out=t["du"], in0=t["delt"], in1=t["usil"][:, dt * TS:(dt + 1) * TS]))
        for gn in range(N16 // NG):
            bld.wait("g", "sp")
            bld.wait("g", "v")
            bld.dma("g", lambda e, gn=gn: e.dma_start(
                out=t["bbc"], in_=d["bcd"][gn * NG:(gn + 1) * NG].rearrange(
                    "(o n) ts -> o (n ts)", o=1).partition_broadcast(128)))
            mcb = bld.dma("g", lambda e, gn=gn: e.dma_start(
                out=t["cbc"], in_=d["bcd"][16 + gn * NG:16 + (gn + 1) * NG].rearrange(
                    "(o n) ts -> o (n ts)", o=1).partition_broadcast(128)))
            bld.wait("g", "g", mcb)
            bld.wait("s", "v", mdu)
            bld.wait("s", "v")
            bld.wait("g", "v", mdu)
            for n in range(NG):
                aap = t["amat"][:, dt * 16 + gn * NG + n:dt * 16 + gn * NG + n + 1]
                bld.op("s", lambda e, n=n, aap=aap: e.activation(
                    out=t["dA"][:, n * TS:(n + 1) * TS], in_=t["delt"],
                    func=AF.Exp, scale=aap))
                bld.op("g", lambda e, n=n: e.tensor_mul(
                    out=t["wv"][:, n * TS:(n + 1) * TS], in0=t["du"],
                    in1=t["bbc"][:, n * TS:(n + 1) * TS]))
            bld.wait("v", "s")
            bld.wait("v", "g")
            for n in range(NG):
                hi = t["hcar"][:, dt * 16 + gn * NG + n:dt * 16 + gn * NG + n + 1]
                bld.op("v", lambda e, n=n, hi=hi: e.tensor_tensor_scan(
                    out=t["hb"][:, n * TS:(n + 1) * TS],
                    data0=t["dA"][:, n * TS:(n + 1) * TS],
                    data1=t["wv"][:, n * TS:(n + 1) * TS],
                    initial=hi, op0=ALU.mult, op1=ALU.add))
            bld.spacer("v")
            bld.op("v", lambda e, dt=dt, gn=gn: e.tensor_copy(
                out=t["hcar"][:, dt * 16 + gn * NG:dt * 16 + (gn + 1) * NG],
                in_=t["hb"].rearrange("p (n ts) -> p n ts", ts=TS)[:, :, TS - 1]))
            bld.wait("g", "v")
            for n in range(NG):
                bld.op("g", lambda e, n=n: e.tensor_mul(
                    out=t["hb"][:, n * TS:(n + 1) * TS],
                    in0=t["hb"][:, n * TS:(n + 1) * TS],
                    in1=t["cbc"][:, n * TS:(n + 1) * TS]))
            bld.wait("v", "g")
            if gn == 0:
                bld.op("v", lambda e: e.tensor_reduce(
                    out=t["yac"], in_=t["hb"].rearrange("p (n ts) -> p ts n", ts=TS),
                    axis=AX.X, op=ALU.add))
            else:
                bld.op("v", lambda e: e.tensor_reduce(
                    out=t["yfin"], in_=t["hb"].rearrange("p (n ts) -> p ts n", ts=TS),
                    axis=AX.X, op=ALU.add))
                bld.op("v", lambda e: e.tensor_add(out=t["yac"], in0=t["yac"], in1=t["yfin"]))
        if DEBUG and s == 0:
            mda = bld.mark("v")
            bld.wait("sp", "v", mda)
            if dt == 0:
                bld.dma("sp", lambda e: e.dma_start(out=d["dbg_amat"][:, 0:128], in_=t["amat"]))
                bld.dma("sp", lambda e: e.dma_start(out=d["dbg_amat"][:, 128:136], in_=t["cns"]))
                bld.dma("sp", lambda e: e.dma_start(out=d["dbg_amat"][:, 136:144],
                                                    in_=t["delt"][:, 0:8]))
                bld.dma("sp", lambda e: e.dma_start(out=d["dbg_scan"][0], in_=t["dA"]))
                bld.dma("sp", lambda e: e.dma_start(out=d["dbg_scan"][1], in_=t["wv"]))
                bld.dma("sp", lambda e: e.dma_start(out=d["dbg_scan"][2], in_=t["hb"]))
                bld.dma("sp", lambda e: e.dma_start(out=d["dbg_scan"][3], in_=t["bbc"]))
                bld.dma("sp", lambda e: e.dma_start(out=d["dbg_scan"][4], in_=t["cbc"]))
                bld.dma("sp", lambda e: e.dma_start(
                    out=d["dbg_scan"][5][:, 0:TS], in_=t["du"]))
            bld.dma("sp", lambda e, dt=dt: e.dma_start(out=d["dbg_yac"][dt], in_=t["yac"]))
        if dt == 0:
            bld.wait("v", "sp")
            bld.wait("v", "t")
        bld.wait("sp", "v")
        mz = bld.dma("sp", lambda e, s=s, dt=dt: e.dma_start(
            out=t["zrl"], in_=d["zsp"][s, dt * 128:(dt + 1) * 128, :]))
        bld.wait("v", "sp", mz)
        dv = t["dvec"][:, dt:dt + 1]
        bld.op("v", lambda e, dt=dt, dv=dv: e.scalar_tensor_tensor(
            out=t["yfin"], in0=t["usil"][:, dt * TS:(dt + 1) * TS], scalar=dv,
            in1=t["yac"], op0=ALU.mult, op1=ALU.add))
        bld.op("v", lambda e, dt=dt: e.tensor_mul(
            out=t["ypub"][:, dt * TS:(dt + 1) * TS], in0=t["yfin"], in1=t["zrl"]))

    bld.wait("sp", "v")
    bld.dma("sp", lambda e, s=s: e.dma_start(out=d["shm_y"][myrow, s], in_=t["ypub"]))
    flag_self(FL_Y, s + 1)
    if DEBUG:
        bld.dma("sp", lambda e, s=s: e.dma_start(out=d["dbg_ntT"][s], in_=t["ntT"]))
        bld.dma("sp", lambda e, s=s: e.dma_start(out=d["dbg_usil"][s], in_=t["usil"]))
        bld.dma("sp", lambda e, s=s: e.dma_start(out=d["dbg_dbc"][s], in_=t["dbc1"]))
        bld.dma("sp", lambda e, s=s: e.dma_start(out=d["dbg_ypub"][s], in_=t["ypub"]))
        bld.dma("sp", lambda e, s=s: e.dma_start(out=d["dbg_delt"][s], in_=t["delt"]))

    if not fin:
        return

    poll_mate(FL_Y, s + 1)
    bld.wait("sp", "t")
    mym = bld.dma("sp", lambda e, s=s: e.dma_start(
        out=t["ymate"], in_=d["shm_y"][materow, s]))
    bld.wait("t", "sp", mym)
    dst_d = d["x1own"] if own else d["x1kv"]
    off = (s - p * 4) if own else (s - q * 4)
    for tt in range(4):
        xr = t["xr0"] if tt % 2 == 0 else t["xr1"]
        ln0 = t["ln0r0"] if tt % 2 == 0 else t["ln0r1"]
        bld.wait("sp", "v")
        bld.wait("sp", "s")
        mx = bld.dma("sp", lambda e, s=s, tt=tt, xr=xr: e.dma_start(
            out=xr, in_=d["x_full"][s * TS + tt * 128:s * TS + (tt + 1) * 128, :]))
        bld.wait("v", "sp", mx)
        ln_1024(xr, ln0, None)
        bld.wait("t", "v")
        for ch in range(2):
            pm = P["E2"] if ch == 0 else P["F2"]
            for gdt in range(16):
                ysrc = t["ypub"] if (gdt // 8) == q else t["ymate"]
                seg = gdt % 8
                bld.op("t", lambda e, tt=tt, ch=ch, gdt=gdt, seg=seg, ysrc=ysrc, pm=pm:
                       e.matmul(
                           out=pm[:],
                           lhsT=ysrc[:, seg * TS + tt * 128:seg * TS + tt * 128 + 128],
                           rhs=t["wout"][:, gdt * 1024 + ch * 512:gdt * 1024 + (ch + 1) * 512],
                           start=(gdt == 0), stop=(gdt == 15)))
        bld.wait("v", "t")
        bld.op("v", lambda e, xr=xr: e.tensor_add(
            out=t["x1a"][:, 0:512], in0=P["E2"][:], in1=xr[:, 0:512]))
        bld.op("v", lambda e, ln0=ln0: e.tensor_add(
            out=t["x1a"][:, 0:512], in0=t["x1a"][:, 0:512], in1=ln0[:, 0:512]))
        bld.op("v", lambda e, xr=xr: e.tensor_add(
            out=t["x1a"][:, 512:1024], in0=P["F2"][:], in1=xr[:, 512:1024]))
        mev = bld.op("v", lambda e, ln0=ln0: e.tensor_add(
            out=t["x1a"][:, 512:1024], in0=t["x1a"][:, 512:1024], in1=ln0[:, 512:1024]))
        bld.wait("t", "v", mev)
        bld.wait("sp", "v", mev)
        bld.dma("sp", lambda e, off=off, tt=tt, dst_d=dst_d: e.dma_start(
            out=dst_d[off * TS + tt * 128:off * TS + (tt + 1) * 128, :], in_=t["x1a"]))


def _ntT_mov(t, cb):
    return t["ntT"][:, cb * TS:(cb + 1) * TS]


def _ntT_dst(t, g0, tt):
    return t["ntT"].rearrange("p (cb ts) -> p cb ts", ts=TS)[
        :, g0:g0 + 4, tt * 128:(tt + 1) * 128]


def _ubf(t):
    return t["ntb"][:, 0:512]


def gate_prep(bld, ctx, sim_d, gate_d):
    t, P, d = ctx["t"], ctx["P"], ctx["d"]
    ones1 = t["stat"][:, 8:9]
    bld.wait("v", "s")
    bld.op("v", lambda e: e.memset(ones1, 1.0))
    bld.wait("sp", "v")
    msim = bld.dma("sp", lambda e: e.dma_start(
        out=t["simsq"].rearrange("p (a c) -> p a c", c=8),
        in_=sim_d.rearrange("(a p) c -> p a c", p=128)))
    bld.wait("v", "sp", msim)
    bld.op("v", lambda e: e.tensor_copy(out=t["simb"], in_=t["simsq"]))
    bld.wait("s", "v")
    ms = bld.op("s", lambda e: e.activation(out=t["simsq"], in_=t["simsq"], func=AF.Square))
    bld.wait("t", "s", ms)
    bld.wait("t", "v")
    for a8 in range(8):
        bld.op("t", lambda e, a8=a8: e.matmul(
            out=P["O"][0:1, 0:8], lhsT=ones1,
            rhs=t["simsq"][:, a8 * 8:(a8 + 1) * 8],
            start=(a8 == 0), stop=(a8 == 7)))
    bld.wait("v", "t")
    bld.tiny("v", lambda e: e.tensor_copy(out=t["simsq"][0:1, 0:8], in_=P["O"][0:1, 0:8]))
    bld.wait("t", "v")
    bld.wait("s", "v")
    bld.tiny("s", lambda e: e.activation(
        out=t["simsq"][0:1, 0:8], in_=t["simsq"][0:1, 0:8], func=AF.Sqrt))
    bld.wait("v", "s")
    bld.tiny("v", lambda e: e.tensor_scalar_max(
        out=t["simsq"][0:1, 0:8], in0=t["simsq"][0:1, 0:8], scalar1=1e-12))
    mv = bld.tiny("v", lambda e: e.reciprocal(
        out=t["simsq"][0:1, 0:8], in_=t["simsq"][0:1, 0:8]))
    bld.wait("sp", "v", mv)
    bld.dma("sp", lambda e: e.dma_start(out=d["g8"][0:8], in_=t["simsq"][0:1, 0:8]))
    mb = bld.dma("sp", lambda e: e.dma_start(
        out=t["simi"], in_=d["g8"][0:8].rearrange("(o c) -> o c", o=1).partition_broadcast(128)))
    mg = bld.dma("sp", lambda e: e.dma_start(
        out=t["sgb"], in_=gate_d.partition_broadcast(128)))
    bld.wait("s", "sp", mg)
    msg = bld.tiny("s", lambda e: e.activation(out=t["sgb"], in_=t["sgb"], func=AF.Sigmoid))
    bld.wait("v", "sp", mb)
    bld.wait("v", "s", msg)


def gating_tt(bld, ctx, tt, sumsq_ap):
    t, P = ctx["t"], ctx["P"]
    bld.wait("t", "v")
    for cb in range(8):
        bld.op("t", lambda e, cb=cb, tt=tt: e.matmul(
            out=P["O"][:, 0:8],
            lhsT=t["nT"][:, cb * 1024 + tt * 128:cb * 1024 + (tt + 1) * 128],
            rhs=t["simb"][:, cb * 8:(cb + 1) * 8],
            start=(cb == 0), stop=(cb == 7)))
    bld.wait("v", "t")
    bld.tiny("v", lambda e: e.tensor_copy(out=t["lg8"], in_=P["O"][:, 0:8]))
    bld.wait("t", "v")
    bld.wait("s", "v")
    bld.tiny("s", lambda e: e.activation(
        out=t["stat"][:, 12:13], in_=sumsq_ap, func=AF.Sqrt))
    bld.wait("v", "s")
    bld.tiny("v", lambda e: e.tensor_scalar_max(
        out=t["stat"][:, 12:13], in0=t["stat"][:, 12:13], scalar1=1e-12))
    bld.tiny("v", lambda e: e.reciprocal(out=t["stat"][:, 12:13], in_=t["stat"][:, 12:13]))
    bld.tiny("v", lambda e: e.tensor_mul(out=t["lg8"], in0=t["lg8"], in1=t["simi"]))
    bld.tiny("v", lambda e: e.tensor_scalar_mul(
        out=t["lg8"], in0=t["lg8"], scalar1=t["stat"][:, 12:13]))
    bld.tiny("v", lambda e: e.tensor_sub(out=t["lg8"], in0=t["lg8"], in1=t["sgb"]))
    bld.tiny("v", lambda e: e.tensor_relu(out=t["mk8"], in_=t["lg8"]))
    bld.tiny("v", lambda e: e.tensor_single_scalar(
        out=t["fb8"], in_=t["mk8"], scalar=0.0, op=ALU.is_gt))
    bld.tiny("v", lambda e: e.tensor_reduce(
        out=t["stat"][:, 13:14], in_=t["fb8"], axis=AX.X, op=ALU.add))
    bld.tiny("v", lambda e: e.tensor_single_scalar(
        out=t["stat"][:, 13:14], in_=t["stat"][:, 13:14], scalar=0.5, op=ALU.is_lt))
    bld.tiny("v", lambda e: e.tensor_reduce(
        out=t["m8"][:, 0:1], in_=t["lg8"], axis=AX.X, op=ALU.max, negate=True))
    bld.tiny("v", lambda e: e.tensor_scalar_add(
        out=t["rw8"], in0=t["lg8"], scalar1=t["m8"][:, 0:1]))
    bld.tiny("v", lambda e: e.tensor_single_scalar(
        out=t["rw8"], in_=t["rw8"], scalar=0.0, op=ALU.is_ge))
    bld.tiny("v", lambda e: e.tensor_scalar_mul(
        out=t["scr8"], in0=t["rw8"], scalar1=1e6))
    bld.tiny("v", lambda e: e.tensor_sub(
        out=t["scr8"], in0=t["lg8"], in1=t["scr8"]))
    bld.tiny("v", lambda e: e.tensor_reduce(
        out=t["m8"][:, 0:1], in_=t["scr8"], axis=AX.X, op=ALU.max, negate=True))
    bld.tiny("v", lambda e: e.tensor_scalar_add(
        out=t["scr8"], in0=t["scr8"], scalar1=t["m8"][:, 0:1]))
    bld.tiny("v", lambda e: e.tensor_single_scalar(
        out=t["scr8"], in_=t["scr8"], scalar=0.0, op=ALU.is_ge))
    bld.tiny("v", lambda e: e.tensor_max(out=t["rw8"], in0=t["rw8"], in1=t["scr8"]))
    bld.tiny("v", lambda e: e.tensor_scalar_mul(
        out=t["rw8"], in0=t["rw8"], scalar1=t["stat"][:, 13:14]))
    bld.tiny("v", lambda e: e.tensor_max(out=t["fb8"], in0=t["fb8"], in1=t["rw8"]))
    bld.tiny("v", lambda e: e.tensor_single_scalar(
        out=t["msk8"], in_=t["fb8"], scalar=0.5, op=ALU.is_gt))
    bld.tiny("v", lambda e: e.select(
        out=t["lg8"], mask=t["msk8"], on_true=t["mk8"], on_false=t["neg8"]))
    mv = bld.tiny("v", lambda e: e.tensor_reduce(
        out=t["stat"][:, 14:15], in_=t["lg8"], axis=AX.X, op=ALU.max, negate=True))
    bld.wait("s", "v", mv)
    bld.wait("s", "t")
    ms = bld.tiny("s", lambda e: e.activation(
        out=t["rw8"], in_=t["lg8"], func=AF.Exp, bias=t["stat"][:, 14:15],
        accum_out=t["stat"][:, 15:16]))
    bld.wait("v", "s", ms)
    bld.tiny("v", lambda e: e.reciprocal(out=t["stat"][:, 15:16], in_=t["stat"][:, 15:16]))
    return bld.tiny("v", lambda e: e.tensor_scalar_mul(
        out=t["rw8"], in0=t["rw8"], scalar1=t["stat"][:, 15:16]))


def ffn_stage1(bld, ctx, p, H, x1src, half, chunk, kv_only, kvcnt):
    t, P, d = ctx["t"], ctx["P"], ctx["d"]
    q = p % 2
    myrow = q
    ln_1024, poll_mate, flag_self = H["ln_1024"], H["poll_mate"], H["flag_self"]
    tof = half * HT

    if half == 0:
        gate_prep(bld, ctx, d["ssim"], d["sgate"])

    for tt in range(8):
        bld.wait("sp", "v")
        bld.wait("sp", "s")
        mx = bld.dma("sp", lambda e, tt=tt: e.dma_start(
            out=t["x1a"], in_=x1src[tof + tt * 128:tof + (tt + 1) * 128, :]))
        bld.wait("v", "sp", mx)
        hsl = t["hs"][:, tt * 1024:(tt + 1) * 1024]
        ln_1024(t["x1a"], hsl, None)
        ln_1024(hsl, t["n1f"], (t["lnw"], t["lnb"]),
                accum_to=t["n1ni"][:, 0:1])
        if not kv_only:
            bld.wait("sp", "v")
            bld.dma("sp", lambda e, tt=tt, hsl=hsl: e.dma_start(
                out=d["xpp"][tof + tt * 128:tof + (tt + 1) * 128, :], in_=hsl))
        mv = bld.op("v", lambda e: e.tensor_copy(out=t["awt"][:, 0:1024], in_=t["n1f"]))
        bld.wait("t", "v", mv)
        for cb in range(8):
            bld.op("t", lambda e, cb=cb: e.transpose(
                out=P["T"][:, (cb % 4) * 128:(cb % 4) * 128 + 128],
                in_=t["awt"][:, cb * 128:(cb + 1) * 128], identity=t["ident"]))
            if cb % 4 == 3:
                bld.wait("v", "t")
                g0 = cb - 3
                bld.op("v", lambda e, g0=g0, tt=tt: e.tensor_copy(
                    out=t["nT"].rearrange("p (cb c) -> p cb c", c=1024)
                    [:, g0:g0 + 4, tt * 128:(tt + 1) * 128],
                    in_=P["T"][:, 0:512]))
                bld.wait("t", "v")
        mrw = gating_tt(bld, ctx, tt, t["n1ni"][:, 0:1])
        bld.wait("t", "v", mrw)
        bld.op("t", lambda e: e.transpose(
            out=P["O"][0:8, 0:128], in_=t["rw8"], identity=t["identf"]))
        bld.wait("v", "t")
        bld.op("v", lambda e, tt=tt: e.tensor_copy(
            out=t["rwTs"][0:8, half * 1024 + tt * 128:half * 1024 + (tt + 1) * 128],
            in_=P["O"][0:8, 0:128]))
        bld.wait("t", "v")
    bld.wait("sp", "v")
    bld.dma("sp", lambda e: e.dma_start(
        out=d["rwTd"][:, half * 1024:(half + 1) * 1024],
        in_=t["rwTs"][0:8, half * 1024:(half + 1) * 1024]))

    projs = [("kf", "K"), ("vf", "V")] + ([] if kv_only else [("qf", "Q")])
    for wname, kind in projs:
        bld.wait("sp", "v")
        mw = bld.dma("sp", lambda e, wname=wname: e.dma_start(
            out=_fstage(t), in_=d[wname]))
        bld.wait("v", "sp", mw)
        bld.op("v", lambda e: e.tensor_copy(out=t["pjw"], in_=_fstage(t)))
        for tch in range(2):
            pm = P["A"] if kind == "K" else (P["B2"] if kind == "Q" else None)
            for j in range(64):
                ee, cb = j // 8, j % 8
                if j % 8 == 0:
                    bld.wait("g", "v")
                    bld.wait("g", "sp")
                    bld.dma("g", lambda e, ee=ee, tch=tch: e.dma_start(
                        out=t["rwbc"],
                        in_=d["rwTd"][ee:ee + 1,
                                      half * 1024 + tch * 512:half * 1024 + (tch + 1) * 512]
                        .partition_broadcast(128)))
                bld.wait("v", "g")
                bld.wait("v", "t")
                bld.op("v", lambda e, cb=cb, tch=tch: e.tensor_mul(
                    out=t["n1wt"],
                    in0=t["nT"][:, cb * 1024 + tch * 512:cb * 1024 + (tch + 1) * 512],
                    in1=t["rwbc"]))
                bld.wait("t", "v")
                if kind in ("K", "Q"):
                    bld.op("t", lambda e, j=j, pm=pm: e.matmul(
                        out=pm[:], lhsT=t["pjw"][:, j * 128:(j + 1) * 128],
                        rhs=t["n1wt"], start=(j == 0), stop=(j == 63)))
                else:
                    for ts4 in range(4):
                        pv = [P["C"], P["D2"], P["E2"], P["F2"]][ts4]
                        bld.op("t", lambda e, j=j, ts4=ts4, pv=pv: e.matmul(
                            out=pv[:, 0:128],
                            lhsT=t["n1wt"][:, ts4 * 128:(ts4 + 1) * 128],
                            rhs=t["pjw"][:, j * 128:(j + 1) * 128],
                            start=(j == 0), stop=(j == 63)))
            bld.wait("v", "t")
            if kind == "K":
                bld.op("v", lambda e, tch=tch: e.tensor_copy(
                    out=t["kT"][:, tch * 512:(tch + 1) * 512], in_=P["A"][:]))
            elif kind == "Q":
                qdst = t["qTa"] if half == 0 else t["qTb"]
                bld.op("v", lambda e, tch=tch, qdst=qdst: e.tensor_copy(
                    out=qdst[:, tch * 512:(tch + 1) * 512], in_=P["B2"][:]))
            else:
                for ts4 in range(4):
                    pv = [P["C"], P["D2"], P["E2"], P["F2"]][ts4]
                    bld.op("v", lambda e, tch=tch, ts4=ts4, pv=pv: e.tensor_copy(
                        out=t["vsb"][:, (tch * 4 + ts4) * 128:(tch * 4 + ts4 + 1) * 128],
                        in_=pv[:, 0:128]))
            bld.wait("t", "v")

    bld.wait("sp", "v")
    if DEBUG and not kv_only:
        bld.dma("sp", lambda e, half=half: e.dma_start(out=d["dbg_kT"][half], in_=t["kT"]))
        bld.dma("sp", lambda e, half=half: e.dma_start(out=d["dbg_vsb"][half], in_=t["vsb"]))
        qsrc = t["qTa"] if half == 0 else t["qTb"]
        bld.dma("sp", lambda e, half=half, qsrc=qsrc: e.dma_start(
            out=d["dbg_qT"][half], in_=qsrc))
    bld.dma("sp", lambda e, chunk=chunk, half=half: e.dma_start(
        out=d["shm_k"][chunk, :, half * 1024:(half + 1) * 1024], in_=t["kT"]))
    bld.dma("sp", lambda e, chunk=chunk, half=half: e.dma_start(
        out=d["shm_v"][chunk, :, half * 1024:(half + 1) * 1024], in_=t["vsb"]))
    kvcnt[0] += 1
    flag_self(FL_KV, kvcnt[0])


def _fstage(t):
    return t["fstgB"]


def ffn_stage2(bld, ctx, p, H, kvcnt):
    t, P, d = ctx["t"], ctx["P"], ctx["d"]
    q = p % 2
    back = p >= 2
    myrow, materow = q, 1 - q
    ln_1024, poll_mate, flag_self = H["ln_1024"], H["poll_mate"], H["flag_self"]

    nch = p + 1
    bld.wait("sp", "v")
    mo = bld.dma("sp", lambda e: e.dma_start(out=_fstage(t), in_=d["of"]))
    bld.wait("v", "sp", mo)
    bld.op("v", lambda e: e.tensor_copy(out=t["pjw"], in_=_fstage(t)))
    bld.dma("sp", lambda e: e.dma_start(out=t["lnw"], in_=d["ln2w"].partition_broadcast(128)))
    mlp = bld.dma("sp", lambda e: e.dma_start(out=t["lnb"], in_=d["ln2b"].partition_broadcast(128)))
    bld.wait("v", "sp", mlp)
    bld.wait("s", "sp", mlp)
    waitcnt = {}
    for ck in range(nch):
        pub = ck % 2
        base = 0 if (not back or ck < 2) else 2
        if pub != myrow:
            waitcnt[ck] = base + 2
    if waitcnt:
        poll_mate(FL_KV, max(waitcnt.values()))
    bld.wait("sp", "v")
    for ck in range(nch):
        bld.dma("sp", lambda e, ck=ck: e.dma_start(
            out=t["kall"][:, ck * 2048:(ck + 1) * 2048], in_=d["shm_k"][ck]))
        bld.dma("sp", lambda e, ck=ck: e.dma_start(
            out=t["vall"][:, ck * 2048:(ck + 1) * 2048], in_=d["shm_v"][ck]))
    mkv = bld.mark("sp")

    bld.wait("g", "v")
    bld.op("g", lambda e: e.memset(t["cmask"], 0.0))
    bld.wait("g", "g")
    for oi in range(4):
        bld.op("g", lambda e, oi=oi: e.affine_select(
            out=t["cmask"][:, oi * 512:(oi + 1) * 512],
            in_=t["cmask"][:, oi * 512:(oi + 1) * 512],
            pattern=[[-1, 512]], compare_op=ALU.is_ge,
            fill=NEG, base=oi * 128, channel_multiplier=1))
    ctx["mmask"] = bld.mark("g")

    for half in range(2):
        tof = half * HT
        qT = t["qTa"] if half == 0 else t["qTb"]
        for tt in range(8):
            bld.wait("sp", "v")
            mx = bld.dma("sp", lambda e, tt=tt, tof=tof: e.dma_start(
                out=t["hs"][:, tt * 1024:(tt + 1) * 1024],
                in_=d["xpp"][tof + tt * 128:tof + (tt + 1) * 128, :]))
        mhs = bld.mark("sp")
        bld.wait("v", "sp", mhs)
        bld.wait("t", "sp", mkv)
        attn_half(bld, ctx, p, half, qT)
        mvh = bld.mark("v")
        bld.wait("sp", "v", mvh)
        for tt in range(8):
            bld.dma("sp", lambda e, tt=tt, tof=tof: e.dma_start(
                out=d["xpp"][tof + tt * 128:tof + (tt + 1) * 128, :],
                in_=t["hs"][:, tt * 1024:(tt + 1) * 1024]))
        if DEBUG:
            for tt in range(8):
                bld.dma("sp", lambda e, tt=tt, half=half: e.dma_start(
                    out=d["dbg_hs"][half * 8 + tt], in_=t["hs"][:, tt * 1024:(tt + 1) * 1024]))

    gate_prep(bld, ctx, d["msim"], d["mgate"])
    for half in range(2):
        tof = half * HT
        for tt in range(8):
            bld.wait("sp", "v")
            bld.wait("sp", "s")
            mx = bld.dma("sp", lambda e, tt=tt, tof=tof: e.dma_start(
                out=t["hs"][:, tt * 1024:(tt + 1) * 1024],
                in_=d["xpp"][tof + tt * 128:tof + (tt + 1) * 128, :]))
        mhs = bld.mark("sp")
        bld.wait("v", "sp", mhs)
        bld.wait("s", "sp", mhs)
        moe_half(bld, ctx, p, half, H)


def attn_half(bld, ctx, p, half, qT):
    t, P, d = ctx["t"], ctx["P"], ctx["d"]
    mmask = ctx["mmask"]
    tof = half * HT
    for qt in range(8):
        q0 = p * TOK + tof + qt * 128
        nkv = (q0 + 128 + 511) // 512
        nbig = (nkv * 512 + 2047) // 2048
        first = True
        for bigc in range(nbig):
            k0 = bigc * 2048
            ncols = min(nkv * 512 - k0, 2048)
            for kvb in range(0, ncols, 512):
                bld.wait("t", "s")
                bld.op("t", lambda e, qt=qt, k0=k0, kvb=kvb, qT=qT: e.matmul(
                    out=P["A"][:], lhsT=qT[:, qt * 128:(qt + 1) * 128],
                    rhs=t["kall"][:, k0 + kvb:k0 + kvb + 512],
                    start=True, stop=True))
                bld.wait("s", "t")
                bld.op("s", lambda e, kvb=kvb: e.activation(
                    out=t["Sb"][:, kvb:kvb + 512], in_=P["A"][:],
                    func=AF.Copy, scale=SCATT))
            bld.wait("v", "s")
            bld.wait("v", "g", mmask)
            for kvb in range(0, ncols, 512):
                cstart = k0 + kvb
                if cstart <= q0 < cstart + 512:
                    oi = (q0 - cstart) // 128
                    bld.op("v", lambda e, kvb=kvb, oi=oi: e.tensor_add(
                        out=t["Sb"][:, kvb:kvb + 512], in0=t["Sb"][:, kvb:kvb + 512],
                        in1=t["cmask"][:, oi * 512:(oi + 1) * 512]))
            bld.tiny("v", lambda e, ncols=ncols: e.tensor_reduce(
                out=t["stat"][:, 6:7], in_=t["Sb"][:, 0:ncols], axis=AX.X,
                op=ALU.max, negate=True))
            if first:
                bld.tiny("v", lambda e: e.tensor_copy(
                    out=t["stat"][:, 4:5], in_=t["stat"][:, 6:7]))
                bld.op("v", lambda e: e.memset(t["stat"][:, 5:6], 0.0))
                bld.op("v", lambda e: e.memset(t["osb"], 0.0))
                bld.spacer("v")
            else:
                bld.tiny("v", lambda e: e.tensor_copy(out=t["stat"][:, 7:8],
                                                      in_=t["stat"][:, 4:5]))
                bld.tiny("v", lambda e: e.tensor_tensor(
                    out=t["stat"][:, 4:5], in0=t["stat"][:, 4:5],
                    in1=t["stat"][:, 6:7], op=ALU.min))
                mvs = bld.tiny("v", lambda e: e.tensor_sub(
                    out=t["stat"][:, 7:8], in0=t["stat"][:, 4:5], in1=t["stat"][:, 7:8]))
                bld.wait("s", "v", mvs)
                mse = bld.tiny("s", lambda e: e.activation(
                    out=t["stat"][:, 7:8], in_=t["stat"][:, 7:8], func=AF.Exp))
                bld.wait("v", "s", mse)
                bld.tiny("v", lambda e: e.tensor_scalar_mul(
                    out=t["stat"][:, 5:6], in0=t["stat"][:, 5:6],
                    scalar1=t["stat"][:, 7:8]))
                bld.tiny("v", lambda e: e.tensor_scalar_mul(
                    out=t["osb"], in0=t["osb"], scalar1=t["stat"][:, 7:8]))
            mvv = bld.mark("v")
            bld.wait("s", "v", mvv)
            ms = bld.tiny("s", lambda e, ncols=ncols: e.activation(
                out=t["Pb"][:, 0:ncols], in_=t["Sb"][:, 0:ncols], func=AF.Exp,
                bias=t["stat"][:, 4:5], accum_out=t["stat"][:, 6:7]))
            bld.wait("v", "s", ms)
            bld.tiny("v", lambda e: e.tensor_add(
                out=t["stat"][:, 5:6], in0=t["stat"][:, 5:6], in1=t["stat"][:, 6:7]))
            nkb = min((q0 + 128 - k0 + 127) // 128, ncols // 128)
            bld.wait("t", "s", ms)
            for kb in range(nkb):
                bld.op("t", lambda e, kb=kb: e.transpose(
                    out=P["T"][:, 0:128], in_=t["Pb"][:, kb * 128:(kb + 1) * 128],
                    identity=t["ident"]))
                bld.wait("v", "t")
                bld.op("v", lambda e: e.tensor_copy(out=t["PTs"], in_=P["T"][:, 0:128]))
                bld.wait("t", "v")
                bld.op("t", lambda e, kb=kb, k0=k0: e.matmul(
                    out=P["O"][:], lhsT=t["PTs"],
                    rhs=t["vall"][:, k0 + kb * 128:k0 + (kb + 1) * 128],
                    start=(kb == 0), stop=(kb == nkb - 1)))
            bld.wait("v", "t")
            bld.op("v", lambda e: e.tensor_add(out=t["osb"], in0=t["osb"], in1=P["O"][:]))
            bld.wait("t", "v")
            first = False
        bld.tiny("v", lambda e: e.reciprocal(out=t["stat"][:, 5:6], in_=t["stat"][:, 5:6]))
        mo2 = bld.op("v", lambda e: e.tensor_scalar_mul(
            out=t["osb"], in0=t["osb"], scalar1=t["stat"][:, 5:6]))
        bld.wait("t", "v", mo2)
        bld.op("t", lambda e: e.transpose(out=P["O"][:], in_=t["osb"], identity=t["identf"]))
        bld.wait("v", "t")
        bld.op("v", lambda e: e.tensor_copy(out=t["otb"], in_=P["O"][:]))
        bld.wait("t", "v")
        for ee in range(8):
            bld.wait("g", "v")
            bld.wait("g", "t")
            mrq = bld.dma("g", lambda e, ee=ee, tof=tof, qt=qt: e.dma_start(
                out=t["rwbc"][:, 0:128],
                in_=d["rwTd"][ee:ee + 1, tof + qt * 128:tof + (qt + 1) * 128]
                .partition_broadcast(128)))
            bld.wait("v", "g", mrq)
            bld.op("v", lambda e, ee=ee: e.tensor_mul(
                out=t["awt"][:, ee * 128:(ee + 1) * 128], in0=t["otb"],
                in1=t["rwbc"][:, 0:128]))
        mawt = bld.mark("v")
        bld.wait("t", "v", mawt)
        for ch in range(2):
            pm = P["E2"] if ch == 0 else P["F2"]
            for ee in range(8):
                bld.op("t", lambda e, ee=ee, ch=ch, pm=pm: e.matmul(
                    out=pm[:], lhsT=t["awt"][:, ee * 128:(ee + 1) * 128],
                    rhs=t["pjw"][:, ee * 1024 + ch * 512:ee * 1024 + (ch + 1) * 512],
                    start=(ee == 0), stop=(ee == 7)))
        bld.wait("v", "t")
        bld.op("v", lambda e, qt=qt: e.tensor_add(
            out=t["hs"][:, qt * 1024:qt * 1024 + 512],
            in0=t["hs"][:, qt * 1024:qt * 1024 + 512], in1=P["E2"][:]))
        bld.op("v", lambda e, qt=qt: e.tensor_add(
            out=t["hs"][:, qt * 1024 + 512:(qt + 1) * 1024],
            in0=t["hs"][:, qt * 1024 + 512:(qt + 1) * 1024], in1=P["F2"][:]))
        bld.wait("t", "v")

    if DEBUG:
        bld.wait("sp", "v")
        for tt in range(8):
            bld.dma("sp", lambda e, tt=tt, half=half: e.dma_start(
                out=d["dbg_hs"][half * 8 + tt], in_=t["hs"][:, tt * 1024:(tt + 1) * 1024]))


def moe_half(bld, ctx, p, half, H):
    t, P, d = ctx["t"], ctx["P"], ctx["d"]
    ln_1024, poll_mate, flag_self = H["ln_1024"], H["poll_mate"], H["flag_self"]
    tof = half * HT
    for tt in range(8):
        hsl = t["hs"][:, tt * 1024:(tt + 1) * 1024]
        ln_1024(hsl, t["n1f"], (t["lnw"], t["lnb"]),
                accum_to=t["n1ni"][:, 0:1])
        mv = bld.op("v", lambda e: e.tensor_copy(out=t["awt"][:, 0:1024], in_=t["n1f"]))
        bld.wait("t", "v", mv)
        for cb in range(8):
            bld.op("t", lambda e, cb=cb: e.transpose(
                out=P["T"][:, (cb % 4) * 128:(cb % 4) * 128 + 128],
                in_=t["awt"][:, cb * 128:(cb + 1) * 128], identity=t["ident"]))
            if cb % 4 == 3:
                bld.wait("v", "t")
                g0 = cb - 3
                bld.op("v", lambda e, g0=g0, tt=tt: e.tensor_copy(
                    out=t["nT"].rearrange("p (cb c) -> p cb c", c=1024)
                    [:, g0:g0 + 4, tt * 128:(tt + 1) * 128],
                    in_=P["T"][:, 0:512]))
                bld.wait("t", "v")
        mrw = gating_tt(bld, ctx, tt, t["n1ni"][:, 0:1])
        if DEBUG and half == 0 and tt == 0:
            bld.wait("sp", "v", mrw)
            bld.wait("sp", "s")
            bld.op("v", lambda e: e.tensor_copy(
                out=t["x1a"][:, 768:832], in_=t["simb"]))
            mdv3 = bld.mark("v")
            bld.wait("sp", "v", mdv3)
            bld.dma("sp", lambda e: e.dma_start(
                out=d["dbg_moe"][0][:, 256:264], in_=t["rw8"]))
            bld.dma("sp", lambda e: e.dma_start(
                out=d["dbg_moe"][0][:, 264:272], in_=t["mk8"]))
            bld.dma("sp", lambda e: e.dma_start(
                out=d["dbg_moe"][0][:, 272:280], in_=t["sgb"]))
            bld.dma("sp", lambda e: e.dma_start(
                out=d["dbg_moe"][0][:, 280:288], in_=t["simi"]))
            bld.dma("sp", lambda e: e.dma_start(
                out=d["dbg_moe"][0][:, 288:296], in_=t["lg8"]))
            bld.dma("sp", lambda e: e.dma_start(
                out=d["dbg_moe"][0][:, 296:304], in_=t["stat"][:, 8:16]))
            bld.dma("sp", lambda e: e.dma_start(
                out=d["dbg_moe"][0][:, 304:368], in_=t["x1a"][:, 768:832]))
            mdd3 = bld.mark("sp")
            bld.wait("v", "sp", mdd3)
            bld.wait("t", "sp", mdd3)
            bld.wait("s", "sp", mdd3)
        bld.wait("t", "v", mrw)
        bld.op("t", lambda e: e.transpose(
            out=P["O"][0:8, 0:128], in_=t["rw8"], identity=t["identf"]))
        bld.wait("v", "t")
        bld.op("v", lambda e, tt=tt, half=half: e.tensor_copy(
            out=t["rwTs"][0:8, half * 1024 + tt * 128:half * 1024 + (tt + 1) * 128],
            in_=P["O"][0:8, 0:128]))
        bld.wait("t", "v")
    bld.wait("sp", "v")
    bld.dma("sp", lambda e, half=half: e.dma_start(
        out=d["rwTd"][:, half * 1024:(half + 1) * 1024],
        in_=t["rwTs"][0:8, half * 1024:(half + 1) * 1024]))
    if DEBUG and half == 0:
        bld.dma("sp", lambda e: e.dma_start(
            out=d["dbg_moe"][3][0:8, 512:1536], in_=t["rwTs"][0:8, 0:1024]))

    for ee in range(8):
        bld.wait("v", "t")
        for wsl, wdd in (("w1c", "w1"), ("w2c", "w2")):
            for hh in range(2):
                bld.wait("sp", "v")
                mw = bld.dma("sp", lambda e, wdd=wdd, ee=ee, hh=hh: e.dma_start(
                    out=_fstage2(t),
                    in_=d[wdd][ee].rearrange("(a p) c -> p a c", p=128)
                    [:, hh * 4:(hh + 1) * 4, :]))
                bld.wait("v", "sp", mw)
                bld.op("v", lambda e, wsl=wsl, hh=hh: e.tensor_copy(
                    out=t[wsl][:, hh * 4096:(hh + 1) * 4096], in_=_fstage2(t)))
        bld.wait("t", "v")
        for tc in range(4):
            bld.wait("g", "v")
            bld.wait("g", "sp")
            mrq = bld.dma("g", lambda e, ee=ee, tc=tc, tof=tof: e.dma_start(
                out=t["rwbc"][:, 0:256],
                in_=d["rwTd"][ee:ee + 1, tof + tc * 256:tof + (tc + 1) * 256]
                .partition_broadcast(128)))
            for it in range(8):
                pm = P["A"] if it % 2 == 0 else P["B2"]
                bld.wait("t", "s")
                for kt in range(8):
                    bld.op("t", lambda e, it=it, kt=kt, tc=tc, pm=pm: e.matmul(
                        out=pm[:, 0:256],
                        lhsT=t["w1c"][:, kt * 1024 + it * 128:kt * 1024 + (it + 1) * 128],
                        rhs=t["nT"][:, kt * 1024 + tc * 256:kt * 1024 + (tc + 1) * 256],
                        start=(kt == 0), stop=(kt == 7)))
                bld.wait("s", "t")
                bld.wait("s", "v")
                bld.op("s", lambda e, pm=pm: e.activation(
                    out=t["gel"], in_=pm[:, 0:256], func=AF.Gelu))
                bld.wait("v", "s")
                if it == 0:
                    bld.wait("v", "g", mrq)
                bld.wait("v", "t")
                bld.op("v", lambda e, it=it: e.tensor_mul(
                    out=t["iwb"][:, it * 256:(it + 1) * 256], in0=t["gel"],
                    in1=t["rwbc"][:, 0:256]))
                if DEBUG and half == 0 and ee == 0 and tc == 1 and it in (2, 4, 6, 7):
                    mdv9 = bld.mark("v")
                    bld.wait("g", "v", mdv9)
                    col = 1792 if it == 7 else 1280 + (it // 2 - 1) * 256
                    mdg9 = bld.dma("g", lambda e, col=col: e.dma_start(
                        out=d["dbg_moe"][7][:, col:col + 256], in_=t["iwb"][:, 0:256]))
                    bld.wait("v", "g", mdg9)
                    bld.wait("t", "g", mdg9)
                    bld.wait("s", "g", mdg9)
                if DEBUG and half == 0 and ee == 0 and tc == 1 and it == 0:
                    bld.op("v", lambda e: e.tensor_copy(
                        out=t["x1a"][:, 0:256], in_=pm[:, 0:256]))
                    bld.op("v", lambda e: e.tensor_copy(
                        out=t["x1a"][:, 256:512], in_=t["gel"]))
                    mdvx = bld.mark("v")
                    bld.wait("sp", "v", mdvx)
                    bld.dma("sp", lambda e: e.dma_start(
                        out=d["dbg_moe"][7][:, 512:1024], in_=t["x1a"][:, 0:512]))
                    bld.wait("g", "v", mdvx)
                    mdgx = bld.dma("g", lambda e: e.dma_start(
                        out=d["dbg_moe"][7][:, 1024:1280], in_=t["iwb"][:, 0:256]))
                    mddx = bld.mark("sp")
                    bld.wait("v", "sp", mddx)
                    bld.wait("t", "sp", mddx)
                    bld.wait("s", "sp", mddx)
                    bld.wait("v", "g", mdgx)
            if DEBUG and half == 0 and ee == 0 and tc <= 1:
                ro = 4 * tc
                mdv = bld.mark("v")
                bld.wait("sp", "v", mdv)
                bld.wait("sp", "s")
                bld.op("v", lambda e: e.tensor_copy(
                    out=t["x1a"][:, 0:256], in_=t["w1c"][:, 0:256]))
                bld.op("v", lambda e, tc=tc: e.tensor_copy(
                    out=t["x1a"][:, 256:512], in_=t["nT"][:, tc * 256:(tc + 1) * 256]))
                mdv2 = bld.mark("v")
                bld.wait("sp", "v", mdv2)
                bld.dma("sp", lambda e, ro=ro: e.dma_start(
                    out=d["dbg_moe"][ro + 0][:, 0:256], in_=t["gel"]))
                bld.wait("g", "v", mdv2)
                mdg = bld.dma("g", lambda e, ro=ro: e.dma_start(
                    out=d["dbg_moe"][ro + 1], in_=t["iwb"]))
                bld.wait("v", "g", mdg)
                bld.dma("sp", lambda e, ro=ro: e.dma_start(
                    out=d["dbg_moe"][ro + 2][:, 0:256], in_=t["rwbc"][:, 0:256]))
                bld.dma("sp", lambda e, ro=ro: e.dma_start(
                    out=d["dbg_moe"][ro + 3][:, 0:512], in_=t["x1a"][:, 0:512]))
                mdd = bld.mark("sp")
                bld.wait("v", "sp", mdd)
                bld.wait("t", "sp", mdd)
                bld.wait("s", "sp", mdd)
            miw = bld.mark("v")
            bld.wait("t", "v", miw)
            for tt2 in range(2):
                for ch in range(2):
                    pm = P["E2"] if ch == 0 else P["F2"]
                    for it in range(8):
                        bld.op("t", lambda e, it=it, tt2=tt2, ch=ch, pm=pm: e.matmul(
                            out=pm[:],
                            lhsT=t["iwb"][:, it * 256 + tt2 * 128:it * 256 + tt2 * 128 + 128],
                            rhs=t["w2c"][:, it * 1024 + ch * 512:it * 1024 + (ch + 1) * 512],
                            start=(it == 0), stop=(it == 7)))
                bld.wait("v", "t")
                ttg = tc * 2 + tt2
                bld.op("v", lambda e, ttg=ttg: e.tensor_add(
                    out=t["hs"][:, ttg * 1024:ttg * 1024 + 512],
                    in0=t["hs"][:, ttg * 1024:ttg * 1024 + 512], in1=P["E2"][:]))
                bld.op("v", lambda e, ttg=ttg: e.tensor_add(
                    out=t["hs"][:, ttg * 1024 + 512:(ttg + 1) * 1024],
                    in0=t["hs"][:, ttg * 1024 + 512:(ttg + 1) * 1024], in1=P["F2"][:]))
                bld.wait("t", "v")

    if DEBUG:
        bld.wait("sp", "v")
        for tt in range(8):
            bld.dma("sp", lambda e, tt=tt, half=half: e.dma_start(
                out=d["dbg_hs2"][half * 8 + tt], in_=t["hs"][:, tt * 1024:(tt + 1) * 1024]))
    for tt in range(8):
        bld.wait("sp", "v")
        mx = bld.dma("sp", lambda e, tt=tt, tof=tof: e.dma_start(
            out=t["x1a"], in_=d["x1own"][tof + tt * 128:tof + (tt + 1) * 128, :]))
        bld.wait("v", "sp", mx)
        mo3 = bld.op("v", lambda e, tt=tt: e.tensor_add(
            out=t["x1a"], in0=t["x1a"], in1=t["hs"][:, tt * 1024:(tt + 1) * 1024]))
        bld.wait("sp", "v", mo3)
        bld.dma("sp", lambda e, tt=tt, tof=tof: e.dma_start(
            out=d["out_x2"][tof + tt * 128:tof + (tt + 1) * 128, :], in_=t["x1a"]))


def _fstage2(t):
    return t["fstgA"]


# version 10
# speedup vs baseline: 1.8386x; 1.8386x over previous
import numpy as np
import concourse.bass as bass
import concourse.mybir as mybir
from concourse.bass_utils import run_bass_kernel_spmd

F32 = mybir.dt.float32
BF16 = mybir.dt.bfloat16
I32 = mybir.dt.int32
U8 = mybir.dt.uint8
AF = mybir.ActivationFunctionType
ALU = mybir.AluOpType
AX = mybir.AxisListType

B = 2
L = 8192
D = 1024
DI = 2048
DH = 1024
N16 = 16
DTR = 64
HD = 128
E8 = 8
TOK = 2048
HT = 1024
TS = 512
NSUB_F = 8
NSUB_B = 16
NG = 2
EPS = 1e-5
SCATT = float(1.0 / np.sqrt(128.0))
NEG = -1e30

ENGS = ("sp", "v", "s", "g", "t")
CONSTS = [1e-5, -0.5, -1.0, 1.0, 1.0 / 1024.0, float(1.0 / np.sqrt(128.0)), 0.0]
FL_DBC, FL_Y, FL_KV = 1, 2, 3
DEBUG = False
LAST_RES = [None]


def kernel(**inp):
    x = np.ascontiguousarray(np.asarray(inp["L0_attn_x"], dtype=np.float32))
    W_in = np.asarray(inp["L0_attn_W_in"], np.float32)
    conv_w = np.asarray(inp["L0_attn_conv_w"], np.float32)
    conv_b = np.asarray(inp["L0_attn_conv_b"], np.float32)
    W_xp = np.asarray(inp["L0_attn_W_xproj"], np.float32)
    W_dt = np.asarray(inp["L0_attn_W_dt"], np.float32)
    b_dt = np.asarray(inp["L0_attn_b_dt"], np.float32)
    A_log = np.asarray(inp["L0_attn_A_log"], np.float32)
    Dv = np.asarray(inp["L0_attn_D"], np.float32)
    W_out = np.asarray(inp["L0_attn_W_out"], np.float32)

    def tr(a, nt):
        m = a.shape[1]
        return np.ascontiguousarray(
            a.reshape(nt, 128, m).transpose(1, 0, 2).reshape(128, nt * m))

    def col8(a):
        return np.ascontiguousarray(a.reshape(8, 128).T)

    in_maps = []
    for c in range(8):
        b, p = c // 4, c % 4
        q = p % 2
        dlo, dhi = q * DH, (q + 1) * DH
        in_maps.append({
            "x_full": x[b],
            "wiu": tr(W_in[:, dlo:dhi], 8),
            "wiz": tr(W_in[:, DI + dlo:DI + dhi], 8),
            "convw": np.ascontiguousarray(
                conv_w[dlo:dhi, 0, :].reshape(8, 128, 4).transpose(1, 0, 2).reshape(128, 32)),
            "convb": col8(conv_b[dlo:dhi]),
            "wxp": tr(W_xp[dlo:dhi], 8),
            "wdt": np.concatenate([np.asarray(W_dt[:, dlo:dhi], np.float32),
                                   np.zeros((128 - DTR, DH), np.float32)], axis=0),
            "bdt": col8(b_dt[dlo:dhi]),
            "alog": tr(A_log[dlo:dhi], 8),
            "dvec": col8(Dv[dlo:dhi]),
            "wout": tr(W_out, 16),
            "lnw": np.asarray(inp["L0_attn_ln_w"], np.float32)[None, :],
            "lnb": np.asarray(inp["L0_attn_ln_b"], np.float32)[None, :],
            "ln1w": np.asarray(inp["L0_ffn_ln1_w"], np.float32)[None, :],
            "ln1b": np.asarray(inp["L0_ffn_ln1_b"], np.float32)[None, :],
            "ln2w": np.asarray(inp["L0_ffn_ln2_w"], np.float32)[None, :],
            "ln2b": np.asarray(inp["L0_ffn_ln2_b"], np.float32)[None, :],
            "ssim": np.asarray(inp["L0_ffn_smha_sim"], np.float32),
            "sgate": np.asarray(inp["L0_ffn_smha_gates"], np.float32)[None, :],
            "msim": np.asarray(inp["L0_ffn_moe_sim"], np.float32),
            "mgate": np.asarray(inp["L0_ffn_moe_gates"], np.float32)[None, :],
            "qf": tr(np.asarray(inp["L0_ffn_q_proj"], np.float32).reshape(E8 * D, HD), 64),
            "kf": tr(np.asarray(inp["L0_ffn_k_proj"], np.float32).reshape(E8 * D, HD), 64),
            "vf": tr(np.asarray(inp["L0_ffn_v_proj"], np.float32).reshape(E8 * D, HD), 64),
            "of": tr(np.asarray(inp["L0_ffn_o_proj"], np.float32).reshape(E8 * HD, D), 8),
            "w1": np.asarray(inp["L0_ffn_w1"], np.float32),
            "w2": np.asarray(inp["L0_ffn_w2"], np.float32),
        })

    nc = build_program()
    res = run_bass_kernel_spmd(nc, in_maps, core_ids=list(range(8)))
    LAST_RES[0] = res
    out = np.empty((B, L, D), dtype=np.float32)
    for c in range(8):
        b, p = c // 4, c % 4
        out[b, p * TOK:(p + 1) * TOK] = res.results[c]["out_x2"]
    return out


class Bld:
    def __init__(self, nc, es):
        self.nc = nc
        self.es = es
        self.ops = {e: [] for e in ENGS}
        self.nreg = 0
        self.rt = {}
        self.E = {e: self.es.enter_context(nc.semaphore("E_" + e)) for e in ENGS}
        self.pollsem = self.es.enter_context(nc.semaphore("E_poll"))
        self.cnt = {e: 0 for e in ENGS}

    def rtag(self):
        self.nreg += 1
        return "r%d" % self.nreg

    def do(self, eng, fn):
        self.ops[eng].append(fn)

    def op(self, eng, fn):
        self.cnt[eng] += 1
        h = self.E[eng]
        self.ops[eng].append(lambda e: fn(e).then_inc(h, 1))
        return self.cnt[eng]

    def dma(self, eng, fn):
        self.cnt[eng] += 16
        h = self.E[eng]
        self.ops[eng].append(lambda e: fn(e).then_inc(h, 16))
        return self.cnt[eng]

    def mark(self, eng):
        return self.cnt[eng]

    def spacer(self, eng):
        j = self.junk[:, 0:16]
        if eng == "s":
            return self.op(eng, lambda e: e.activation(out=j, in_=j, func=AF.Identity))
        return self.op(eng, lambda e: e.tensor_copy(out=j, in_=j))

    def tiny(self, eng, fn):
        self.op(eng, fn)
        return self.spacer(eng)

    def wait(self, waiter, producer, val=None):
        v = self.cnt[producer] if val is None else val
        if v <= 0:
            return
        h = self.E[producer]
        self.ops[waiter].append(lambda e: e.wait_ge(h, v))

    def barrier(self):
        for e in ENGS:
            self.op(e, lambda en: en.nop())
        marks = {e: self.cnt[e] for e in ENGS}
        for w in ENGS:
            for pd in ENGS:
                if w != pd:
                    self.wait(w, pd, marks[pd])

    def open_pid_reg(self, eng):
        rt = self.rt

        def go(e):
            cm = e.register("preg_" + eng)
            r = cm.__enter__()
            pid = e.partition_id()
            e.reg_alu(r, pid, 3, ALU.bitwise_and)
            rt[("preg", eng)] = (r, cm)
            if eng == "sp":
                cm2 = e.register("pollcnt")
                r2 = cm2.__enter__()
                e.reg_mov(r2, 16)
                rt[("pollcnt",)] = (r2, cm2)
        self.ops[eng].append(go)

    def open_if_p(self, eng, pval):
        rt = self.rt

        def go(e):
            gd = e.If_eq(rt[("preg", eng)][0], pval)
            gd.__enter__()
            rt[("guard", eng)] = gd
        self.ops[eng].append(go)

    def close_if(self, eng):
        rt = self.rt

        def go(e):
            rt[("guard", eng)].__exit__(None, None, None)
        self.ops[eng].append(go)

    def poll(self, eng, ftile, fslot, fsrc, want):
        assert eng == "sp"
        tag = self.rtag()
        ph = self.pollsem
        h = self.E[eng]
        rt = self.rt
        self.cnt[eng] += 1

        def go(e):
            it = rt[("pollcnt",)][0]
            with e.register("pf" + tag) as fv, e.register("pc" + tag) as cnd:
                e.reg_mov(fv, -1)
                e.reg_alu(cnd, fv, want, ALU.is_lt)
                with e.While(cnd):
                    e.dma_start(out=ftile[0:1, fslot:fslot + 1], in_=fsrc).then_inc(ph, 16)
                    e.reg_add(it, it, 16)
                    e.wait_ge(ph, it)
                    e.reg_load(fv, ftile[0:1, fslot:fslot + 1])
                    e.reg_alu(cnd, fv, want, ALU.is_lt)
            return e.sem_inc(h, 1)
        self.ops[eng].append(go)


class Arena:

    def __init__(self, base_ap, words):
        self.base = base_ap
        self.words = words

    def f32(self, off, cols, parts=128):
        assert off + cols <= self.words, (off, cols)
        return self.base[0:parts, off:off + cols]

    def bf16(self, off, cols, parts=128):
        assert off + (cols + 1) // 2 <= self.words, (off, cols)
        return self.base.bitcast(BF16)[0:parts, 2 * off:2 * off + cols]

    def i32(self, off, cols, parts=128):
        assert off + cols <= self.words, (off, cols)
        return self.base.bitcast(I32)[0:parts, off:off + cols]


def build_program():
    nc = bass.Bass("TRN2", target_bir_lowering=False)
    from contextlib import ExitStack
    es = ExitStack()
    bld = Bld(nc, es)

    d = {}
    for nm, sh in [
        ("x_full", [L, D]), ("wiu", [128, 8192]), ("wiz", [128, 8192]),
        ("convw", [128, 32]), ("convb", [128, 8]), ("wxp", [128, 8 * 96]),
        ("wdt", [128, DH]), ("bdt", [128, 8]), ("alog", [128, 128]),
        ("dvec", [128, 8]), ("wout", [128, 16384]),
        ("lnw", [1, D]), ("lnb", [1, D]), ("ln1w", [1, D]), ("ln1b", [1, D]),
        ("ln2w", [1, D]), ("ln2b", [1, D]),
        ("ssim", [D, E8]), ("sgate", [1, E8]), ("msim", [D, E8]), ("mgate", [1, E8]),
        ("qf", [128, 8192]), ("kf", [128, 8192]), ("vf", [128, 8192]),
        ("of", [128, 8192]), ("w1", [E8, D, 1024]), ("w2", [E8, 1024, D]),
    ]:
        d[nm] = nc.dram_tensor(nm, sh, F32, kind="ExternalInput").ap()
    d["out_x2"] = nc.dram_tensor("out_x2", [TOK, D], F32, kind="ExternalOutput").ap()

    dbgk = dict(kind="ExternalOutput") if DEBUG else {}
    d["zsp"] = nc.dram_tensor("zsp", [NSUB_B, DH, TS], F32, **dbgk).ap()
    d["x1own"] = nc.dram_tensor("x1own", [TOK, D], F32, **dbgk).ap()
    d["x1kv"] = nc.dram_tensor("x1kv", [TOK, D], F32, **dbgk).ap()
    d["xpp"] = nc.dram_tensor("xpp", [TOK, D], F32, **dbgk).ap()
    d["bcd"] = nc.dram_tensor("bcd", [32, TS], F32).ap()
    d["rwTd"] = nc.dram_tensor("rwTd", [E8, TOK], F32,
                               **(dict(kind="ExternalOutput") if DEBUG else {})).ap()
    d["g8"] = nc.dram_tensor("g8", [16], F32).ap()

    if DEBUG:
        d["dbg_ntT"] = nc.dram_tensor("dbg_ntT", [NSUB_B, 128, 4096], BF16,
                                      kind="ExternalOutput").ap()
        d["dbg_usil"] = nc.dram_tensor("dbg_usil", [NSUB_B, 128, 4096], F32,
                                       kind="ExternalOutput").ap()
        d["dbg_dbc"] = nc.dram_tensor("dbg_dbc", [NSUB_B, 96, TS], F32,
                                      kind="ExternalOutput").ap()
        d["dbg_ypub"] = nc.dram_tensor("dbg_ypub", [NSUB_B, 128, 4096], BF16,
                                       kind="ExternalOutput").ap()
        d["dbg_delt"] = nc.dram_tensor("dbg_delt", [NSUB_B, 128, TS], F32,
                                       kind="ExternalOutput").ap()
    if DEBUG:
        d["dbg_scan"] = nc.dram_tensor("dbg_scan", [6, 128, NG * TS], F32,
                                       kind="ExternalOutput").ap()
        d["dbg_amat"] = nc.dram_tensor("dbg_amat", [128, 144], F32,
                                       kind="ExternalOutput").ap()
        d["dbg_yac"] = nc.dram_tensor("dbg_yac", [8, 128, TS], F32,
                                      kind="ExternalOutput").ap()
        d["dbg_kT"] = nc.dram_tensor("dbg_kT", [2, 128, 1024], BF16,
                                     kind="ExternalOutput").ap()
        d["dbg_vsb"] = nc.dram_tensor("dbg_vsb", [2, 128, 1024], BF16,
                                      kind="ExternalOutput").ap()
        d["dbg_qT"] = nc.dram_tensor("dbg_qT", [2, 128, 1024], BF16,
                                     kind="ExternalOutput").ap()
        d["dbg_hs"] = nc.dram_tensor("dbg_hs", [16, 128, 1024], F32,
                                     kind="ExternalOutput").ap()
        d["dbg_hs2"] = nc.dram_tensor("dbg_hs2", [16, 128, 1024], F32,
                                      kind="ExternalOutput").ap()
        d["dbg_moe"] = nc.dram_tensor("dbg_moe", [8, 128, 2048], F32,
                                      kind="ExternalOutput").ap()
    d["shm_dbc"] = nc.dram_tensor("shm_dbc", [2, NSUB_B, 96, TS], F32,
                                  kind="Internal", addr_space="Shared").ap()
    d["shm_y"] = nc.dram_tensor("shm_y", [2, NSUB_B, 128, 8 * TS], BF16,
                                kind="Internal", addr_space="Shared").ap()
    d["shm_k"] = nc.dram_tensor("shm_k", [4, 128, TOK], BF16,
                                kind="Internal", addr_space="Shared").ap()
    d["shm_v"] = nc.dram_tensor("shm_v", [4, 128, 16 * HD], BF16,
                                kind="Internal", addr_space="Shared").ap()
    d["shm_flg"] = nc.dram_tensor("shm_flg", [2, 64], I32,
                                  kind="Internal", addr_space="Shared").ap()

    AW = 48000
    with es:
        arena_t = es.enter_context(nc.sbuf_tensor("arena", [128, AW], F32))
        A = Arena(arena_t[:], AW)
        t, PS = layout(A, nc, es)
        bld.junk = t["junk"]
        for i, cv in enumerate(CONSTS):
            nc.const_aps.register(F32, cv, t["cns"][:, i:i + 1])
        ctx = dict(t=t, P=PS, d=d, A=A)
        boot(bld, ctx)
        snap = dict(bld.cnt)
        for eng in ENGS:
            bld.open_pid_reg(eng)
        for p in range(4):
            for eng in ENGS:
                bld.open_if_p(eng, p)
            bld.cnt = dict(snap)
            klass(bld, ctx, p)
            for eng in ENGS:
                bld.close_if(eng)

        with nc.Block() as block:
            @block.sync
            def _(e):
                for fn in bld.ops["sp"]:
                    fn(e)

            @block.vector
            def _(e):
                for fn in bld.ops["v"]:
                    fn(e)

            @block.scalar
            def _(e):
                for fn in bld.ops["s"]:
                    fn(e)

            @block.gpsimd
            def _(e):
                for fn in bld.ops["g"]:
                    fn(e)

            @block.tensor
            def _(e):
                for fn in bld.ops["t"]:
                    fn(e)
    return nc


def layout(A, nc, es):
    t = {}
    o = 0

    def take(words):
        nonlocal o
        r = o
        o += words
        return r

    t["amat"] = A.f32(take(128), 128)
    t["convw"] = A.f32(take(32), 32)
    t["convb"] = A.f32(take(8), 8)
    t["bdt"] = A.f32(take(8), 8)
    t["dvec"] = A.f32(take(8), 8)
    t["wdt"] = A.bf16(take(512), 1024, parts=64)
    t["wxp"] = A.bf16(take(384), 768)
    t["lnw"] = A.f32(take(1024), 1024)
    t["lnb"] = A.f32(take(1024), 1024)
    t["ident"] = A.bf16(take(64), 128)
    t["identf"] = A.f32(take(128), 128)
    t["ci"] = A.i32(take(64), 64, parts=1)
    t["zi"] = A.i32(take(64), 64, parts=1)
    t["ftile"] = A.i32(take(16), 16, parts=1)
    t["neg8"] = A.f32(take(8), 8)
    t["stat"] = A.f32(take(16), 16)
    t["cns"] = A.f32(take(8), 8)
    t["junk"] = A.f32(take(256), 256)
    t["sq"] = A.f32(take(1024), 1024)
    t["x1a"] = A.f32(take(1024), 1024)
    t["hcar"] = A.f32(take(128), 128)
    pers_end = o

    o = pers_end
    t["wiu"] = A.bf16(take(4096), 8192)
    t["wiz"] = A.bf16(take(4096), 8192)
    t["wout"] = A.bf16(take(8192), 16384)
    t["xr0"] = A.f32(take(1024), 1024)
    t["xr1"] = A.f32(take(1024), 1024)
    t["ln0r0"] = A.f32(take(1024), 1024)
    t["ln0r1"] = A.f32(take(1024), 1024)
    t["ntb"] = A.bf16(take(512), 1024)
    t["ntT"] = A.bf16(take(2048), 4096)
    t["urx"] = A.f32(take(2 * (TS + 4)), 2 * (TS + 4))
    t["utail"] = A.f32(take(32), 32)
    t["usil"] = A.f32(take(4096), 4096)
    zq_off = take(512)
    t["zq"] = A.f32(zq_off, 512)
    t["sp2"] = A.f32(zq_off, 512)
    t["dbc1"] = A.f32(take(512), 512, parts=96)
    delt_off = take(512)
    t["delt"] = A.f32(delt_off, 512)
    t["dbc2"] = A.f32(delt_off, 512, parts=96)
    t["dtbf"] = A.bf16(take(256), 512, parts=64)
    t["du"] = A.f32(take(512), 512)
    t["dA"] = A.f32(take(NG * 512), NG * 512)
    t["wv"] = A.f32(take(NG * 512), NG * 512)
    t["hb"] = A.f32(take(NG * 512), NG * 512)
    t["bbc"] = A.f32(take(NG * 512), NG * 512)
    t["cbc"] = A.f32(take(NG * 512), NG * 512)
    t["yac"] = A.f32(take(512), 512)
    t["yfin"] = A.f32(take(512), 512)
    t["ypub"] = A.bf16(take(2048), 4096)
    t["ymate"] = A.bf16(take(2048), 4096)
    t["zrl"] = A.f32(take(512), 512)
    assert o <= A.words, ("mamba", o)

    t["stg"] = A.f32(pers_end + 16384, 8192)

    o = pers_end
    t["hs"] = A.f32(take(8 * 1024), 8192)
    t["nT"] = A.bf16(take(4096), 8192)
    pjw_off = take(4096)
    t["pjw"] = A.bf16(pjw_off, 8192)
    t["fstgA"] = A.f32(pjw_off, 4096)
    t["qTa"] = A.bf16(take(512), 1024)
    t["qTb"] = A.bf16(take(512), 1024)
    t["kT"] = A.bf16(take(512), 1024)
    t["vsb"] = A.bf16(take(512), 1024)
    kall_off = take(4096)
    t["kall"] = A.bf16(kall_off, 8192)
    vall_off = take(4096)
    t["vall"] = A.bf16(vall_off, 8192)
    assert vall_off == kall_off + 4096
    t["fstgB"] = A.f32(kall_off, 8192)
    t["w1c"] = t["kall"]
    t["w2c"] = t["vall"]
    t["Sb"] = A.f32(take(2048), 2048)
    t["Pb"] = A.bf16(take(1024), 2048)
    t["n1f"] = A.f32(take(1024), 1024)
    t["rwTs"] = A.f32(take(2048), 2048, parts=8)
    t["rwbc"] = A.f32(take(512), 512)
    t["n1wt"] = A.bf16(take(256), 512)
    t["PTs"] = A.bf16(take(256), 512)
    t["osb"] = A.f32(take(128), 128)
    t["otb"] = A.f32(take(128), 128)
    t["awt"] = A.bf16(take(512), 1024)
    t["gel"] = A.f32(take(512), 512)
    t["iwb"] = A.bf16(take(2048), 4096)
    t["simb"] = A.bf16(take(32), 64)
    t["simi"] = A.f32(take(8), 8)
    t["sgb"] = A.f32(take(8), 8)
    t["n1ni"] = A.f32(take(8), 8)
    t["rw8"] = A.f32(take(8), 8)
    t["lg8"] = A.f32(take(8), 8)
    t["mk8"] = A.f32(take(8), 8)
    t["fb8"] = A.f32(take(8), 8)
    t["m8"] = A.f32(take(8), 8)
    t["scr8"] = A.f32(take(8), 8)
    msk_off = take(8)
    t["msk8"] = A.base.bitcast(U8)[0:128, 4 * msk_off:4 * msk_off + 8]
    t["simsq"] = A.f32(take(64), 64)
    t["cmask"] = A.f32(take(2048), 2048)
    assert o <= A.words, ("ffn", o)

    PS = {}

    def psum(name, shape, dtype=F32):
        return es.enter_context(nc.psum_tensor(name, shape, dtype))

    PS["A"] = psum("pA", [128, TS])
    PS["B2"] = psum("pB2", [128, TS])
    PS["C"] = psum("pC", [128, TS])
    PS["D2"] = psum("pD2", [128, TS])
    PS["E2"] = psum("pE2", [128, TS])
    PS["F2"] = psum("pF2", [128, TS])
    PS["T"] = psum("pT", [128, 512], BF16)
    PS["O"] = psum("pO", [128, 128])
    return t, PS


def boot(bld, ctx):
    t, P, d = ctx["t"], ctx["P"], ctx["d"]
    stage = t["stg"]

    def mkconst_a(e):
        e.memset(t["ident"], 1.0)
        e.memset(t["identf"], 1.0)
        e.memset(t["zi"], 0)
        e.memset(t["hcar"], 0.0)
        e.memset(t["neg8"], NEG)
        for i, cv in enumerate(CONSTS):
            e.memset(t["cns"][:, i:i + 1], cv)
        return e.iota(t["ci"], pattern=[[1, 64]], base=1, channel_multiplier=0)

    def mkconst_b(e):
        e.affine_select(out=t["ident"], in_=t["ident"], pattern=[[-1, 128]],
                        compare_op=ALU.is_equal, fill=0.0, base=0, channel_multiplier=1)
        return e.affine_select(out=t["identf"], in_=t["identf"], pattern=[[-1, 128]],
                               compare_op=ALU.is_equal, fill=0.0, base=0,
                               channel_multiplier=1)
    bld.op("g", mkconst_a)
    bld.wait("g", "g")
    mg0 = bld.op("g", mkconst_b)
    loads = [
        (t["wiu"][:, 0:8192], d["wiu"], 8192),
        (t["wiz"][:, 0:8192], d["wiz"], 8192),
        (t["wout"][:, 0:8192], d["wout"][:, 0:8192], 8192),
        (t["wout"][:, 8192:16384], d["wout"][:, 8192:16384], 8192),
        (t["wxp"][:, 0:768], d["wxp"], 768),
    ]
    for (dst, src, colsn) in loads:
        bld.wait("sp", "v")
        m = bld.dma("sp", lambda e, src=src, c=colsn: e.dma_start(
            out=stage[:, 0:c], in_=src))
        bld.wait("v", "sp", m)
        bld.op("v", lambda e, dst=dst, c=colsn: e.tensor_copy(
            out=dst, in_=stage[:, 0:c]))
    bld.wait("sp", "v")
    m = bld.dma("sp", lambda e: e.dma_start(out=stage[:, 0:DH], in_=d["wdt"]))
    bld.wait("sp", "sp")
    bld.wait("v", "sp", m)
    bld.op("v", lambda e: e.tensor_copy(out=t["wdt"], in_=stage[0:DTR, 0:DH]))
    bld.wait("g", "v")
    bld.wait("g", "sp")
    bld.op("g", lambda e: e.memset(t["urx"], 0.0))
    bld.op("g", lambda e: e.memset(t["utail"], 0.0))
    for (dst, src) in [(t["convw"], d["convw"]), (t["convb"], d["convb"]),
                       (t["bdt"], d["bdt"]), (t["dvec"], d["dvec"]),
                       (t["amat"], d["alog"])]:
        bld.dma("sp", lambda e, dd=dst, s2=src: e.dma_start(out=dd, in_=s2))
    for nm, src in (("lnw", "lnw"), ("lnb", "lnb")):
        bld.dma("sp", lambda e, nm=nm, src=src: e.dma_start(
            out=t[nm], in_=d[src].partition_broadcast(128)))
    bld.wait("s", "sp")
    bld.wait("s", "g", mg0)
    bld.op("s", lambda e: e.activation(out=t["amat"], in_=t["amat"], func=AF.Exp))
    bld.op("s", lambda e: e.mul(out=t["amat"], in_=t["amat"], mul=-1.0))


    zsem = bld.pollsem

    def zfl(e):
        tag = bld.rtag()
        pid = e.partition_id()
        with e.register("zq" + tag) as qr:
            e.reg_alu(qr, pid, 1, ALU.bitwise_and)
            with e.If_eq(qr, 0):
                e.dma_start(out=d["shm_flg"][0:1, :], in_=t["zi"]).then_inc(zsem, 16)
            with e.Else():
                e.dma_start(out=d["shm_flg"][1:2, :], in_=t["zi"]).then_inc(zsem, 16)
        return e.wait_ge(zsem, 16)
    bld.wait("sp", "g")
    bld.op("sp", zfl)
    bld.barrier()


def klass(bld, ctx, p):
    t, P, d = ctx["t"], ctx["P"], ctx["d"]
    q = p % 2
    back = p >= 2
    NS = NSUB_B if back else NSUB_F
    myrow, materow = q, 1 - q

    def ln_1024(src_ap, dst_ap, affine, accum_to=None):
        nmu = t["stat"][:, 0:1]
        var = t["stat"][:, 1:2]
        rstd = t["stat"][:, 2:3]
        bld.tiny("v", lambda e: e.tensor_reduce(
            out=nmu, in_=src_ap, axis=AX.X, op=ALU.add, negate=True))
        bld.tiny("v", lambda e: e.tensor_scalar_mul(out=nmu, in0=nmu, scalar1=1.0 / D))
        bld.wait("s", "v")
        bld.tiny("s", lambda e: e.activation(
            out=t["sq"], in_=src_ap, func=AF.Square, bias=nmu, accum_out=var))
        bld.tiny("s", lambda e: e.activation(
            out=rstd, in_=var, func=AF.Ln, scale=1.0 / D, bias=EPS))
        ms = bld.tiny("s", lambda e: e.activation(out=rstd, in_=rstd, func=AF.Exp, scale=-0.5))
        bld.wait("v", "s", ms)
        if affine is None:
            mv = bld.op("v", lambda e: e.tensor_scalar(
                out=dst_ap, in0=src_ap, scalar1=nmu, scalar2=rstd,
                op0=ALU.add, op1=ALU.mult))
        else:
            wt, bt = affine
            bld.op("v", lambda e: e.tensor_scalar(
                out=t["sq"], in0=src_ap, scalar1=nmu, scalar2=rstd,
                op0=ALU.add, op1=ALU.mult))
            bld.op("v", lambda e: e.tensor_mul(out=t["sq"], in0=t["sq"], in1=wt))
            mv = bld.op("v", lambda e: e.tensor_add(out=dst_ap, in0=t["sq"], in1=bt))
        if accum_to is not None:
            bld.wait("s", "v", mv)
            bld.op("s", lambda e: e.activation(
                out=t["sq"], in_=dst_ap, func=AF.Square, accum_out=accum_to))
        return mv

    def poll_mate(slot, want):
        bld.poll("sp", t["ftile"], slot,
                 d["shm_flg"][materow:materow + 1, slot:slot + 1], want)

    def flag_self(slot, val):
        bld.wait("sp", "sp")
        bld.dma("sp", lambda e: e.dma_start(
            out=d["shm_flg"][myrow:myrow + 1, slot:slot + 1],
            in_=t["ci"][0:1, val - 1:val]))

    H = dict(ln_1024=ln_1024, poll_mate=poll_mate, flag_self=flag_self)

    for s in range(NS):
        mamba_subtile(bld, ctx, p, s, H)

    bld.barrier()
    bld.dma("sp", lambda e: e.dma_start(out=t["lnw"], in_=d["ln1w"].partition_broadcast(128)))
    mlp1 = bld.dma("sp", lambda e: e.dma_start(out=t["lnb"], in_=d["ln1b"].partition_broadcast(128)))
    bld.wait("v", "sp", mlp1)
    bld.wait("s", "sp", mlp1)

    kvcnt = [0]
    if back:
        for h in range(2):
            ffn_stage1(bld, ctx, p, H, x1src=d["x1kv"], half=h, chunk=q,
                       kv_only=True, kvcnt=kvcnt)
    for h in range(2):
        ffn_stage1(bld, ctx, p, H, x1src=d["x1own"], half=h, chunk=p,
                   kv_only=False, kvcnt=kvcnt)
    bld.barrier()
    ffn_stage2(bld, ctx, p, H, kvcnt)

    for w in ENGS:
        bld.wait(w, "sp")


def mamba_subtile(bld, ctx, p, s, H):
    t, P, d = ctx["t"], ctx["P"], ctx["d"]
    q = p % 2
    back = p >= 2
    own = (p * 4) <= s < (p * 4 + 4)
    kv = back and (q * 4) <= s < (q * 4 + 4)
    fin = own or kv
    myrow, materow = q, 1 - q
    ln_1024, poll_mate, flag_self = H["ln_1024"], H["poll_mate"], H["flag_self"]

    for tt in range(4):
        xr = t["xr0"] if tt % 2 == 0 else t["xr1"]
        ln0 = t["ln0r0"] if tt % 2 == 0 else t["ln0r1"]
        bld.wait("sp", "v")
        bld.wait("sp", "s")
        mx = bld.dma("sp", lambda e, s=s, tt=tt, xr=xr: e.dma_start(
            out=xr, in_=d["x_full"][s * TS + tt * 128:s * TS + (tt + 1) * 128, :]))
        bld.wait("v", "sp", mx)
        ln_1024(xr, ln0, None)
        ln_1024(ln0, t["ntb"], (t["lnw"], t["lnb"]))
        bld.wait("t", "v")
        for cb in range(8):
            bld.op("t", lambda e, cb=cb: e.transpose(
                out=P["T"][:, (cb % 4) * 128:(cb % 4) * 128 + 128],
                in_=t["ntb"][:, cb * 128:(cb + 1) * 128], identity=t["ident"]))
            if cb % 4 == 3:
                bld.wait("v", "t")
                g0 = cb - 3
                bld.op("v", lambda e, g0=g0, tt=tt: e.tensor_copy(
                    out=_ntT_dst(t, g0, tt), in_=P["T"][:, 0:512]))
                bld.wait("t", "v")

    for dt in range(8):
        pm = P["A"] if dt % 2 == 0 else P["B2"]
        bld.wait("t", "v")
        bld.wait("t", "s")
        for cb in range(8):
            bld.op("t", lambda e, cb=cb, dt=dt, pm=pm: e.matmul(
                out=pm[:], lhsT=t["wiu"][:, cb * 1024 + dt * 128:cb * 1024 + dt * 128 + 128],
                rhs=_ntT_mov(t, cb), start=(cb == 0), stop=(cb == 7)))
        seg = t["urx"][:, (dt % 2) * (TS + 4):(dt % 2 + 1) * (TS + 4)]
        bld.wait("v", "t")
        bld.op("v", lambda e, pm=pm, seg=seg: e.tensor_copy(out=seg[:, 4:4 + TS], in_=pm[:]))
        bld.wait("t", "v")
        bld.op("v", lambda e, seg=seg, dt=dt: e.tensor_copy(
            out=seg[:, 1:4], in_=t["utail"][:, dt * 4:dt * 4 + 3]))
        acc = t["du"]
        bld.wait("v", "s")
        bld.op("v", lambda e, seg=seg, dt=dt, acc=acc: e.tensor_scalar(
            out=acc, in0=seg[:, 4:4 + TS], scalar1=t["convw"][:, dt * 4 + 3:dt * 4 + 4],
            scalar2=None, op0=ALU.mult))
        for k in range(3):
            bld.op("v", lambda e, seg=seg, dt=dt, k=k, acc=acc: e.scalar_tensor_tensor(
                out=acc, in0=seg[:, 1 + k:1 + k + TS],
                scalar=t["convw"][:, dt * 4 + k:dt * 4 + k + 1], in1=acc,
                op0=ALU.mult, op1=ALU.add))
        bld.op("v", lambda e, seg=seg, dt=dt: e.tensor_copy(
            out=t["utail"][:, dt * 4:dt * 4 + 3], in_=seg[:, 1 + TS:4 + TS]))
        bld.wait("s", "v")
        bld.op("s", lambda e, dt=dt, acc=acc: e.activation(
            out=t["usil"][:, dt * TS:(dt + 1) * TS], in_=acc, func=AF.Silu,
            bias=t["convb"][:, dt:dt + 1]))
        pm2 = P["C"] if dt % 2 == 0 else P["D2"]
        bld.wait("t", "s")
        for cb in range(8):
            bld.op("t", lambda e, cb=cb, dt=dt, pm2=pm2: e.matmul(
                out=pm2[:], lhsT=t["wiz"][:, cb * 1024 + dt * 128:cb * 1024 + dt * 128 + 128],
                rhs=_ntT_mov(t, cb), start=(cb == 0), stop=(cb == 7)))
        bld.wait("s", "t")
        bld.wait("s", "sp")
        bld.op("s", lambda e, pm2=pm2: e.activation(out=t["zq"], in_=pm2[:], func=AF.Silu))
        bld.wait("sp", "s")
        bld.dma("sp", lambda e, s=s, dt=dt: e.dma_start(
            out=d["zsp"][s, dt * 128:(dt + 1) * 128, :], in_=t["zq"]))

    for dt in range(8):
        bld.wait("v", "s")
        bld.wait("v", "t")
        bld.op("v", lambda e, dt=dt: e.tensor_copy(
            out=_ubf(t), in_=t["usil"][:, dt * TS:(dt + 1) * TS]))
        bld.wait("t", "v")
        bld.op("t", lambda e, dt=dt: e.matmul(
            out=P["E2"][0:96, :], lhsT=t["wxp"][:, dt * 96:(dt + 1) * 96],
            rhs=_ubf(t), start=(dt == 0), stop=(dt == 7)))
    bld.wait("v", "t")
    bld.op("v", lambda e: e.tensor_copy(out=t["dbc1"], in_=P["E2"][0:96, :]))
    bld.wait("t", "v")
    bld.wait("sp", "v")
    bld.dma("sp", lambda e, s=s: e.dma_start(out=d["shm_dbc"][myrow, s], in_=t["dbc1"]))
    flag_self(FL_DBC, s + 1)
    poll_mate(FL_DBC, s + 1)
    mdbc = bld.dma("sp", lambda e, s=s: e.dma_start(
        out=t["dbc2"], in_=d["shm_dbc"][materow, s]))
    bld.wait("v", "sp", mdbc)
    bld.op("v", lambda e: e.tensor_add(out=t["dbc1"], in0=t["dbc1"], in1=t["dbc2"]))
    mtb = bld.op("v", lambda e: e.tensor_copy(out=t["dtbf"], in_=t["dbc1"][0:64, :]))
    bld.wait("sp", "v", mtb)
    bld.dma("sp", lambda e: e.dma_start(out=d["bcd"][:], in_=t["dbc1"][64:96, :]))

    for dt in range(8):
        bld.wait("t", "v")
        bld.op("t", lambda e, dt=dt: e.matmul(
            out=P["F2"][:], lhsT=t["wdt"][:, dt * 128:(dt + 1) * 128],
            rhs=t["dtbf"], start=True, stop=True))
        bb = t["bdt"][:, dt:dt + 1]
        bld.wait("s", "t")
        bld.wait("s", "v")
        bld.op("s", lambda e, bb=bb: e.activation(
            out=t["sp2"], in_=P["F2"][:], func=AF.Abs, bias=bb))
        bld.op("s", lambda e: e.activation(out=t["sp2"], in_=t["sp2"], func=AF.Exp, scale=-1.0))
        bld.op("s", lambda e: e.activation(out=t["sp2"], in_=t["sp2"], func=AF.Ln, bias=1.0))
        ms = bld.op("s", lambda e, bb=bb: e.activation(
            out=t["delt"], in_=P["F2"][:], func=AF.Relu, bias=bb))
        bld.wait("v", "s", ms)
        bld.op("v", lambda e: e.tensor_add(out=t["delt"], in0=t["delt"], in1=t["sp2"]))
        bld.wait("t", "v")
        mdu = bld.op("v", lambda e, dt=dt: e.tensor_mul(
            out=t["du"], in0=t["delt"], in1=t["usil"][:, dt * TS:(dt + 1) * TS]))
        for gn in range(N16 // NG):
            bld.wait("g", "sp")
            bld.wait("g", "v")
            bld.dma("g", lambda e, gn=gn: e.dma_start(
                out=t["bbc"], in_=d["bcd"][gn * NG:(gn + 1) * NG].rearrange(
                    "(o n) ts -> o (n ts)", o=1).partition_broadcast(128)))
            mcb = bld.dma("g", lambda e, gn=gn: e.dma_start(
                out=t["cbc"], in_=d["bcd"][16 + gn * NG:16 + (gn + 1) * NG].rearrange(
                    "(o n) ts -> o (n ts)", o=1).partition_broadcast(128)))
            bld.wait("g", "g", mcb)
            bld.wait("s", "v", mdu)
            bld.wait("s", "v")
            bld.wait("g", "v", mdu)
            for n in range(NG):
                aap = t["amat"][:, dt * 16 + gn * NG + n:dt * 16 + gn * NG + n + 1]
                bld.op("s", lambda e, n=n, aap=aap: e.activation(
                    out=t["dA"][:, n * TS:(n + 1) * TS], in_=t["delt"],
                    func=AF.Exp, scale=aap))
                bld.op("g", lambda e, n=n: e.tensor_mul(
                    out=t["wv"][:, n * TS:(n + 1) * TS], in0=t["du"],
                    in1=t["bbc"][:, n * TS:(n + 1) * TS]))
            bld.wait("v", "s")
            bld.wait("v", "g")
            for n in range(NG):
                hi = t["hcar"][:, dt * 16 + gn * NG + n:dt * 16 + gn * NG + n + 1]
                bld.op("v", lambda e, n=n, hi=hi: e.tensor_tensor_scan(
                    out=t["hb"][:, n * TS:(n + 1) * TS],
                    data0=t["dA"][:, n * TS:(n + 1) * TS],
                    data1=t["wv"][:, n * TS:(n + 1) * TS],
                    initial=hi, op0=ALU.mult, op1=ALU.add))
            bld.spacer("v")
            bld.op("v", lambda e, dt=dt, gn=gn: e.tensor_copy(
                out=t["hcar"][:, dt * 16 + gn * NG:dt * 16 + (gn + 1) * NG],
                in_=t["hb"].rearrange("p (n ts) -> p n ts", ts=TS)[:, :, TS - 1]))
            bld.wait("g", "v")
            for n in range(NG):
                bld.op("g", lambda e, n=n: e.tensor_mul(
                    out=t["hb"][:, n * TS:(n + 1) * TS],
                    in0=t["hb"][:, n * TS:(n + 1) * TS],
                    in1=t["cbc"][:, n * TS:(n + 1) * TS]))
            bld.wait("v", "g")
            if gn == 0:
                bld.op("v", lambda e: e.tensor_reduce(
                    out=t["yac"], in_=t["hb"].rearrange("p (n ts) -> p ts n", ts=TS),
                    axis=AX.X, op=ALU.add))
            else:
                bld.op("v", lambda e: e.tensor_reduce(
                    out=t["yfin"], in_=t["hb"].rearrange("p (n ts) -> p ts n", ts=TS),
                    axis=AX.X, op=ALU.add))
                bld.op("v", lambda e: e.tensor_add(out=t["yac"], in0=t["yac"], in1=t["yfin"]))
        if DEBUG and s == 0:
            mda = bld.mark("v")
            bld.wait("sp", "v", mda)
            if dt == 0:
                bld.dma("sp", lambda e: e.dma_start(out=d["dbg_amat"][:, 0:128], in_=t["amat"]))
                bld.dma("sp", lambda e: e.dma_start(out=d["dbg_amat"][:, 128:136], in_=t["cns"]))
                bld.dma("sp", lambda e: e.dma_start(out=d["dbg_amat"][:, 136:144],
                                                    in_=t["delt"][:, 0:8]))
                bld.dma("sp", lambda e: e.dma_start(out=d["dbg_scan"][0], in_=t["dA"]))
                bld.dma("sp", lambda e: e.dma_start(out=d["dbg_scan"][1], in_=t["wv"]))
                bld.dma("sp", lambda e: e.dma_start(out=d["dbg_scan"][2], in_=t["hb"]))
                bld.dma("sp", lambda e: e.dma_start(out=d["dbg_scan"][3], in_=t["bbc"]))
                bld.dma("sp", lambda e: e.dma_start(out=d["dbg_scan"][4], in_=t["cbc"]))
                bld.dma("sp", lambda e: e.dma_start(
                    out=d["dbg_scan"][5][:, 0:TS], in_=t["du"]))
            bld.dma("sp", lambda e, dt=dt: e.dma_start(out=d["dbg_yac"][dt], in_=t["yac"]))
        if dt == 0:
            bld.wait("v", "sp")
            bld.wait("v", "t")
        bld.wait("sp", "v")
        mz = bld.dma("sp", lambda e, s=s, dt=dt: e.dma_start(
            out=t["zrl"], in_=d["zsp"][s, dt * 128:(dt + 1) * 128, :]))
        bld.wait("v", "sp", mz)
        dv = t["dvec"][:, dt:dt + 1]
        bld.op("v", lambda e, dt=dt, dv=dv: e.scalar_tensor_tensor(
            out=t["yfin"], in0=t["usil"][:, dt * TS:(dt + 1) * TS], scalar=dv,
            in1=t["yac"], op0=ALU.mult, op1=ALU.add))
        bld.op("v", lambda e, dt=dt: e.tensor_mul(
            out=t["ypub"][:, dt * TS:(dt + 1) * TS], in0=t["yfin"], in1=t["zrl"]))

    bld.wait("sp", "v")
    bld.dma("sp", lambda e, s=s: e.dma_start(out=d["shm_y"][myrow, s], in_=t["ypub"]))
    flag_self(FL_Y, s + 1)
    if DEBUG:
        bld.dma("sp", lambda e, s=s: e.dma_start(out=d["dbg_ntT"][s], in_=t["ntT"]))
        bld.dma("sp", lambda e, s=s: e.dma_start(out=d["dbg_usil"][s], in_=t["usil"]))
        bld.dma("sp", lambda e, s=s: e.dma_start(out=d["dbg_dbc"][s], in_=t["dbc1"]))
        bld.dma("sp", lambda e, s=s: e.dma_start(out=d["dbg_ypub"][s], in_=t["ypub"]))
        bld.dma("sp", lambda e, s=s: e.dma_start(out=d["dbg_delt"][s], in_=t["delt"]))

    if not fin:
        return

    poll_mate(FL_Y, s + 1)
    bld.wait("sp", "t")
    mym = bld.dma("sp", lambda e, s=s: e.dma_start(
        out=t["ymate"], in_=d["shm_y"][materow, s]))
    bld.wait("t", "sp", mym)
    dst_d = d["x1own"] if own else d["x1kv"]
    off = (s - p * 4) if own else (s - q * 4)
    for tt in range(4):
        xr = t["xr0"] if tt % 2 == 0 else t["xr1"]
        ln0 = t["ln0r0"] if tt % 2 == 0 else t["ln0r1"]
        bld.wait("sp", "v")
        bld.wait("sp", "s")
        mx = bld.dma("sp", lambda e, s=s, tt=tt, xr=xr: e.dma_start(
            out=xr, in_=d["x_full"][s * TS + tt * 128:s * TS + (tt + 1) * 128, :]))
        bld.wait("v", "sp", mx)
        ln_1024(xr, ln0, None)
        bld.wait("t", "v")
        for ch in range(2):
            pm = P["E2"] if ch == 0 else P["F2"]
            for gdt in range(16):
                ysrc = t["ypub"] if (gdt // 8) == q else t["ymate"]
                seg = gdt % 8
                bld.op("t", lambda e, tt=tt, ch=ch, gdt=gdt, seg=seg, ysrc=ysrc, pm=pm:
                       e.matmul(
                           out=pm[:],
                           lhsT=ysrc[:, seg * TS + tt * 128:seg * TS + tt * 128 + 128],
                           rhs=t["wout"][:, gdt * 1024 + ch * 512:gdt * 1024 + (ch + 1) * 512],
                           start=(gdt == 0), stop=(gdt == 15)))
        bld.wait("v", "t")
        bld.op("v", lambda e, xr=xr: e.tensor_add(
            out=t["x1a"][:, 0:512], in0=P["E2"][:], in1=xr[:, 0:512]))
        bld.op("v", lambda e, ln0=ln0: e.tensor_add(
            out=t["x1a"][:, 0:512], in0=t["x1a"][:, 0:512], in1=ln0[:, 0:512]))
        bld.op("v", lambda e, xr=xr: e.tensor_add(
            out=t["x1a"][:, 512:1024], in0=P["F2"][:], in1=xr[:, 512:1024]))
        mev = bld.op("v", lambda e, ln0=ln0: e.tensor_add(
            out=t["x1a"][:, 512:1024], in0=t["x1a"][:, 512:1024], in1=ln0[:, 512:1024]))
        bld.wait("t", "v", mev)
        bld.wait("sp", "v", mev)
        bld.dma("sp", lambda e, off=off, tt=tt, dst_d=dst_d: e.dma_start(
            out=dst_d[off * TS + tt * 128:off * TS + (tt + 1) * 128, :], in_=t["x1a"]))


def _ntT_mov(t, cb):
    return t["ntT"][:, cb * TS:(cb + 1) * TS]


def _ntT_dst(t, g0, tt):
    return t["ntT"].rearrange("p (cb ts) -> p cb ts", ts=TS)[
        :, g0:g0 + 4, tt * 128:(tt + 1) * 128]


def _ubf(t):
    return t["ntb"][:, 0:512]


def gate_prep(bld, ctx, sim_d, gate_d):
    t, P, d = ctx["t"], ctx["P"], ctx["d"]
    ones1 = t["stat"][:, 8:9]
    bld.wait("v", "s")
    bld.op("v", lambda e: e.memset(ones1, 1.0))
    bld.wait("sp", "v")
    msim = bld.dma("sp", lambda e: e.dma_start(
        out=t["simsq"].rearrange("p (a c) -> p a c", c=8),
        in_=sim_d.rearrange("(a p) c -> p a c", p=128)))
    bld.wait("v", "sp", msim)
    bld.op("v", lambda e: e.tensor_copy(out=t["simb"], in_=t["simsq"]))
    bld.wait("s", "v")
    ms = bld.op("s", lambda e: e.activation(out=t["simsq"], in_=t["simsq"], func=AF.Square))
    bld.wait("t", "s", ms)
    bld.wait("t", "v")
    for a8 in range(8):
        bld.op("t", lambda e, a8=a8: e.matmul(
            out=P["O"][0:1, 0:8], lhsT=ones1,
            rhs=t["simsq"][:, a8 * 8:(a8 + 1) * 8],
            start=(a8 == 0), stop=(a8 == 7)))
    bld.wait("v", "t")
    bld.tiny("v", lambda e: e.tensor_copy(out=t["simsq"][0:1, 0:8], in_=P["O"][0:1, 0:8]))
    bld.wait("t", "v")
    bld.wait("s", "v")
    bld.tiny("s", lambda e: e.activation(
        out=t["simsq"][0:1, 0:8], in_=t["simsq"][0:1, 0:8], func=AF.Sqrt))
    bld.wait("v", "s")
    bld.tiny("v", lambda e: e.tensor_scalar_max(
        out=t["simsq"][0:1, 0:8], in0=t["simsq"][0:1, 0:8], scalar1=1e-12))
    mv = bld.tiny("v", lambda e: e.reciprocal(
        out=t["simsq"][0:1, 0:8], in_=t["simsq"][0:1, 0:8]))
    bld.wait("sp", "v", mv)
    bld.dma("sp", lambda e: e.dma_start(out=d["g8"][0:8], in_=t["simsq"][0:1, 0:8]))
    mb = bld.dma("sp", lambda e: e.dma_start(
        out=t["simi"], in_=d["g8"][0:8].rearrange("(o c) -> o c", o=1).partition_broadcast(128)))
    mg = bld.dma("sp", lambda e: e.dma_start(
        out=t["sgb"], in_=gate_d.partition_broadcast(128)))
    bld.wait("s", "sp", mg)
    msg = bld.tiny("s", lambda e: e.activation(out=t["sgb"], in_=t["sgb"], func=AF.Sigmoid))
    bld.wait("v", "sp", mb)
    bld.wait("v", "s", msg)


def gating_tt(bld, ctx, tt, sumsq_ap):
    t, P = ctx["t"], ctx["P"]
    bld.wait("t", "v")
    for cb in range(8):
        bld.op("t", lambda e, cb=cb, tt=tt: e.matmul(
            out=P["O"][:, 0:8],
            lhsT=t["nT"][:, cb * 1024 + tt * 128:cb * 1024 + (tt + 1) * 128],
            rhs=t["simb"][:, cb * 8:(cb + 1) * 8],
            start=(cb == 0), stop=(cb == 7)))
    bld.wait("v", "t")
    bld.tiny("v", lambda e: e.tensor_copy(out=t["lg8"], in_=P["O"][:, 0:8]))
    bld.wait("t", "v")
    bld.wait("s", "v")
    bld.tiny("s", lambda e: e.activation(
        out=t["stat"][:, 12:13], in_=sumsq_ap, func=AF.Sqrt))
    bld.wait("v", "s")
    bld.tiny("v", lambda e: e.tensor_scalar_max(
        out=t["stat"][:, 12:13], in0=t["stat"][:, 12:13], scalar1=1e-12))
    bld.tiny("v", lambda e: e.reciprocal(out=t["stat"][:, 12:13], in_=t["stat"][:, 12:13]))
    bld.tiny("v", lambda e: e.tensor_mul(out=t["lg8"], in0=t["lg8"], in1=t["simi"]))
    bld.tiny("v", lambda e: e.tensor_scalar_mul(
        out=t["lg8"], in0=t["lg8"], scalar1=t["stat"][:, 12:13]))
    bld.tiny("v", lambda e: e.tensor_sub(out=t["lg8"], in0=t["lg8"], in1=t["sgb"]))
    bld.tiny("v", lambda e: e.tensor_relu(out=t["mk8"], in_=t["lg8"]))
    bld.tiny("v", lambda e: e.tensor_single_scalar(
        out=t["fb8"], in_=t["mk8"], scalar=0.0, op=ALU.is_gt))
    bld.tiny("v", lambda e: e.tensor_reduce(
        out=t["stat"][:, 13:14], in_=t["fb8"], axis=AX.X, op=ALU.add))
    bld.tiny("v", lambda e: e.tensor_single_scalar(
        out=t["stat"][:, 13:14], in_=t["stat"][:, 13:14], scalar=0.5, op=ALU.is_lt))
    bld.tiny("v", lambda e: e.tensor_reduce(
        out=t["m8"][:, 0:1], in_=t["lg8"], axis=AX.X, op=ALU.max, negate=True))
    bld.tiny("v", lambda e: e.tensor_scalar_add(
        out=t["rw8"], in0=t["lg8"], scalar1=t["m8"][:, 0:1]))
    bld.tiny("v", lambda e: e.tensor_single_scalar(
        out=t["rw8"], in_=t["rw8"], scalar=0.0, op=ALU.is_ge))
    bld.tiny("v", lambda e: e.tensor_scalar_mul(
        out=t["scr8"], in0=t["rw8"], scalar1=1e6))
    bld.tiny("v", lambda e: e.tensor_sub(
        out=t["scr8"], in0=t["lg8"], in1=t["scr8"]))
    bld.tiny("v", lambda e: e.tensor_reduce(
        out=t["m8"][:, 0:1], in_=t["scr8"], axis=AX.X, op=ALU.max, negate=True))
    bld.tiny("v", lambda e: e.tensor_scalar_add(
        out=t["scr8"], in0=t["scr8"], scalar1=t["m8"][:, 0:1]))
    bld.tiny("v", lambda e: e.tensor_single_scalar(
        out=t["scr8"], in_=t["scr8"], scalar=0.0, op=ALU.is_ge))
    bld.tiny("v", lambda e: e.tensor_max(out=t["rw8"], in0=t["rw8"], in1=t["scr8"]))
    bld.tiny("v", lambda e: e.tensor_scalar_mul(
        out=t["rw8"], in0=t["rw8"], scalar1=t["stat"][:, 13:14]))
    bld.tiny("v", lambda e: e.tensor_max(out=t["fb8"], in0=t["fb8"], in1=t["rw8"]))
    bld.tiny("v", lambda e: e.tensor_single_scalar(
        out=t["msk8"], in_=t["fb8"], scalar=0.5, op=ALU.is_gt))
    bld.tiny("v", lambda e: e.select(
        out=t["lg8"], mask=t["msk8"], on_true=t["mk8"], on_false=t["neg8"]))
    mv = bld.tiny("v", lambda e: e.tensor_reduce(
        out=t["stat"][:, 14:15], in_=t["lg8"], axis=AX.X, op=ALU.max, negate=True))
    bld.wait("s", "v", mv)
    bld.wait("s", "t")
    ms = bld.tiny("s", lambda e: e.activation(
        out=t["rw8"], in_=t["lg8"], func=AF.Exp, bias=t["stat"][:, 14:15],
        accum_out=t["stat"][:, 15:16]))
    bld.wait("v", "s", ms)
    bld.tiny("v", lambda e: e.reciprocal(out=t["stat"][:, 15:16], in_=t["stat"][:, 15:16]))
    return bld.tiny("v", lambda e: e.tensor_scalar_mul(
        out=t["rw8"], in0=t["rw8"], scalar1=t["stat"][:, 15:16]))


def ffn_stage1(bld, ctx, p, H, x1src, half, chunk, kv_only, kvcnt):
    t, P, d = ctx["t"], ctx["P"], ctx["d"]
    q = p % 2
    myrow = q
    ln_1024, poll_mate, flag_self = H["ln_1024"], H["poll_mate"], H["flag_self"]
    tof = half * HT

    if half == 0:
        gate_prep(bld, ctx, d["ssim"], d["sgate"])

    for tt in range(8):
        bld.wait("sp", "v")
        bld.wait("sp", "s")
        mx = bld.dma("sp", lambda e, tt=tt: e.dma_start(
            out=t["x1a"], in_=x1src[tof + tt * 128:tof + (tt + 1) * 128, :]))
        bld.wait("v", "sp", mx)
        hsl = t["hs"][:, tt * 1024:(tt + 1) * 1024]
        ln_1024(t["x1a"], hsl, None)
        ln_1024(hsl, t["n1f"], (t["lnw"], t["lnb"]),
                accum_to=t["n1ni"][:, 0:1])
        if not kv_only:
            bld.wait("sp", "v")
            bld.dma("sp", lambda e, tt=tt, hsl=hsl: e.dma_start(
                out=d["xpp"][tof + tt * 128:tof + (tt + 1) * 128, :], in_=hsl))
        mv = bld.op("v", lambda e: e.tensor_copy(out=t["awt"][:, 0:1024], in_=t["n1f"]))
        bld.wait("t", "v", mv)
        for cb in range(8):
            bld.op("t", lambda e, cb=cb: e.transpose(
                out=P["T"][:, (cb % 4) * 128:(cb % 4) * 128 + 128],
                in_=t["awt"][:, cb * 128:(cb + 1) * 128], identity=t["ident"]))
            if cb % 4 == 3:
                bld.wait("v", "t")
                g0 = cb - 3
                bld.op("v", lambda e, g0=g0, tt=tt: e.tensor_copy(
                    out=t["nT"].rearrange("p (cb c) -> p cb c", c=1024)
                    [:, g0:g0 + 4, tt * 128:(tt + 1) * 128],
                    in_=P["T"][:, 0:512]))
                bld.wait("t", "v")
        mrw = gating_tt(bld, ctx, tt, t["n1ni"][:, 0:1])
        bld.wait("t", "v", mrw)
        bld.op("t", lambda e: e.transpose(
            out=P["O"][0:8, 0:128], in_=t["rw8"], identity=t["identf"]))
        bld.wait("v", "t")
        bld.op("v", lambda e, tt=tt: e.tensor_copy(
            out=t["rwTs"][0:8, half * 1024 + tt * 128:half * 1024 + (tt + 1) * 128],
            in_=P["O"][0:8, 0:128]))
        bld.wait("t", "v")
    bld.wait("sp", "v")
    bld.dma("sp", lambda e: e.dma_start(
        out=d["rwTd"][:, half * 1024:(half + 1) * 1024],
        in_=t["rwTs"][0:8, half * 1024:(half + 1) * 1024]))

    projs = [("kf", "K"), ("vf", "V")] + ([] if kv_only else [("qf", "Q")])
    for wname, kind in projs:
        bld.wait("sp", "v")
        mw = bld.dma("sp", lambda e, wname=wname: e.dma_start(
            out=_fstage(t), in_=d[wname]))
        bld.wait("v", "sp", mw)
        bld.op("v", lambda e: e.tensor_copy(out=t["pjw"], in_=_fstage(t)))
        for tch in range(2):
            pm = P["A"] if kind == "K" else (P["B2"] if kind == "Q" else None)
            for j in range(64):
                ee, cb = j // 8, j % 8
                if j % 8 == 0:
                    bld.wait("g", "v")
                    bld.wait("g", "sp")
                    bld.dma("g", lambda e, ee=ee, tch=tch: e.dma_start(
                        out=t["rwbc"],
                        in_=d["rwTd"][ee:ee + 1,
                                      half * 1024 + tch * 512:half * 1024 + (tch + 1) * 512]
                        .partition_broadcast(128)))
                bld.wait("v", "g")
                bld.wait("v", "t")
                bld.op("v", lambda e, cb=cb, tch=tch: e.tensor_mul(
                    out=t["n1wt"],
                    in0=t["nT"][:, cb * 1024 + tch * 512:cb * 1024 + (tch + 1) * 512],
                    in1=t["rwbc"]))
                bld.wait("t", "v")
                if kind in ("K", "Q"):
                    bld.op("t", lambda e, j=j, pm=pm: e.matmul(
                        out=pm[:], lhsT=t["pjw"][:, j * 128:(j + 1) * 128],
                        rhs=t["n1wt"], start=(j == 0), stop=(j == 63)))
                else:
                    for ts4 in range(4):
                        pv = [P["C"], P["D2"], P["E2"], P["F2"]][ts4]
                        bld.op("t", lambda e, j=j, ts4=ts4, pv=pv: e.matmul(
                            out=pv[:, 0:128],
                            lhsT=t["n1wt"][:, ts4 * 128:(ts4 + 1) * 128],
                            rhs=t["pjw"][:, j * 128:(j + 1) * 128],
                            start=(j == 0), stop=(j == 63)))
            bld.wait("v", "t")
            if kind == "K":
                bld.op("v", lambda e, tch=tch: e.tensor_copy(
                    out=t["kT"][:, tch * 512:(tch + 1) * 512], in_=P["A"][:]))
            elif kind == "Q":
                qdst = t["qTa"] if half == 0 else t["qTb"]
                bld.op("v", lambda e, tch=tch, qdst=qdst: e.tensor_copy(
                    out=qdst[:, tch * 512:(tch + 1) * 512], in_=P["B2"][:]))
            else:
                for ts4 in range(4):
                    pv = [P["C"], P["D2"], P["E2"], P["F2"]][ts4]
                    bld.op("v", lambda e, tch=tch, ts4=ts4, pv=pv: e.tensor_copy(
                        out=t["vsb"][:, (tch * 4 + ts4) * 128:(tch * 4 + ts4 + 1) * 128],
                        in_=pv[:, 0:128]))
            bld.wait("t", "v")

    bld.wait("sp", "v")
    if DEBUG and not kv_only:
        bld.dma("sp", lambda e, half=half: e.dma_start(out=d["dbg_kT"][half], in_=t["kT"]))
        bld.dma("sp", lambda e, half=half: e.dma_start(out=d["dbg_vsb"][half], in_=t["vsb"]))
        qsrc = t["qTa"] if half == 0 else t["qTb"]
        bld.dma("sp", lambda e, half=half, qsrc=qsrc: e.dma_start(
            out=d["dbg_qT"][half], in_=qsrc))
    bld.dma("sp", lambda e, chunk=chunk, half=half: e.dma_start(
        out=d["shm_k"][chunk, :, half * 1024:(half + 1) * 1024], in_=t["kT"]))
    bld.dma("sp", lambda e, chunk=chunk, half=half: e.dma_start(
        out=d["shm_v"][chunk, :, half * 1024:(half + 1) * 1024], in_=t["vsb"]))
    kvcnt[0] += 1
    flag_self(FL_KV, kvcnt[0])


def _fstage(t):
    return t["fstgB"]


def ffn_stage2(bld, ctx, p, H, kvcnt):
    t, P, d = ctx["t"], ctx["P"], ctx["d"]
    q = p % 2
    back = p >= 2
    myrow, materow = q, 1 - q
    ln_1024, poll_mate, flag_self = H["ln_1024"], H["poll_mate"], H["flag_self"]

    nch = p + 1
    bld.wait("sp", "v")
    mo = bld.dma("sp", lambda e: e.dma_start(out=_fstage(t), in_=d["of"]))
    bld.wait("v", "sp", mo)
    bld.op("v", lambda e: e.tensor_copy(out=t["pjw"], in_=_fstage(t)))
    bld.dma("sp", lambda e: e.dma_start(out=t["lnw"], in_=d["ln2w"].partition_broadcast(128)))
    mlp = bld.dma("sp", lambda e: e.dma_start(out=t["lnb"], in_=d["ln2b"].partition_broadcast(128)))
    bld.wait("v", "sp", mlp)
    bld.wait("s", "sp", mlp)
    waitcnt = {}
    for ck in range(nch):
        pub = ck % 2
        base = 0 if (not back or ck < 2) else 2
        if pub != myrow:
            waitcnt[ck] = base + 2
    if waitcnt:
        poll_mate(FL_KV, max(waitcnt.values()))
    bld.wait("sp", "v")
    for ck in range(nch):
        bld.dma("sp", lambda e, ck=ck: e.dma_start(
            out=t["kall"][:, ck * 2048:(ck + 1) * 2048], in_=d["shm_k"][ck]))
        bld.dma("sp", lambda e, ck=ck: e.dma_start(
            out=t["vall"][:, ck * 2048:(ck + 1) * 2048], in_=d["shm_v"][ck]))
    mkv = bld.mark("sp")

    bld.wait("g", "v")
    bld.op("g", lambda e: e.memset(t["cmask"], 0.0))
    bld.wait("g", "g")
    for oi in range(4):
        bld.op("g", lambda e, oi=oi: e.affine_select(
            out=t["cmask"][:, oi * 512:(oi + 1) * 512],
            in_=t["cmask"][:, oi * 512:(oi + 1) * 512],
            pattern=[[-1, 512]], compare_op=ALU.is_ge,
            fill=NEG, base=oi * 128, channel_multiplier=1))
    ctx["mmask"] = bld.mark("g")

    for half in range(2):
        tof = half * HT
        qT = t["qTa"] if half == 0 else t["qTb"]
        for tt in range(8):
            bld.wait("sp", "v")
            mx = bld.dma("sp", lambda e, tt=tt, tof=tof: e.dma_start(
                out=t["hs"][:, tt * 1024:(tt + 1) * 1024],
                in_=d["xpp"][tof + tt * 128:tof + (tt + 1) * 128, :]))
        mhs = bld.mark("sp")
        bld.wait("v", "sp", mhs)
        bld.wait("t", "sp", mkv)
        attn_half(bld, ctx, p, half, qT)
        mvh = bld.mark("v")
        bld.wait("sp", "v", mvh)
        for tt in range(8):
            bld.dma("sp", lambda e, tt=tt, tof=tof: e.dma_start(
                out=d["xpp"][tof + tt * 128:tof + (tt + 1) * 128, :],
                in_=t["hs"][:, tt * 1024:(tt + 1) * 1024]))
        if DEBUG:
            for tt in range(8):
                bld.dma("sp", lambda e, tt=tt, half=half: e.dma_start(
                    out=d["dbg_hs"][half * 8 + tt], in_=t["hs"][:, tt * 1024:(tt + 1) * 1024]))

    gate_prep(bld, ctx, d["msim"], d["mgate"])
    for half in range(2):
        tof = half * HT
        for tt in range(8):
            bld.wait("sp", "v")
            bld.wait("sp", "s")
            mx = bld.dma("sp", lambda e, tt=tt, tof=tof: e.dma_start(
                out=t["hs"][:, tt * 1024:(tt + 1) * 1024],
                in_=d["xpp"][tof + tt * 128:tof + (tt + 1) * 128, :]))
        mhs = bld.mark("sp")
        bld.wait("v", "sp", mhs)
        bld.wait("s", "sp", mhs)
        moe_half(bld, ctx, p, half, H)


def attn_half(bld, ctx, p, half, qT):
    t, P, d = ctx["t"], ctx["P"], ctx["d"]
    mmask = ctx["mmask"]
    tof = half * HT
    smark = None
    for qt in range(8):
        q0 = p * TOK + tof + qt * 128
        nkv = (q0 + 128 + 511) // 512
        nbig = (nkv * 512 + 2047) // 2048
        first = True
        for bigc in range(nbig):
            k0 = bigc * 2048
            ncols = min(nkv * 512 - k0, 2048)
            for kvb in range(0, ncols, 512):
                if kvb == 0 and smark is not None:
                    bld.wait("t", "s", smark)
                else:
                    bld.wait("t", "s")
                bld.op("t", lambda e, qt=qt, k0=k0, kvb=kvb, qT=qT: e.matmul(
                    out=P["A"][:], lhsT=qT[:, qt * 128:(qt + 1) * 128],
                    rhs=t["kall"][:, k0 + kvb:k0 + kvb + 512],
                    start=True, stop=True))
                bld.wait("s", "t")
                bld.op("s", lambda e, kvb=kvb: e.activation(
                    out=t["Sb"][:, kvb:kvb + 512], in_=P["A"][:],
                    func=AF.Copy, scale=SCATT))
            smark = bld.mark("s")
            bld.wait("v", "s")
            bld.wait("v", "g", mmask)
            for kvb in range(0, ncols, 512):
                cstart = k0 + kvb
                if cstart <= q0 < cstart + 512:
                    oi = (q0 - cstart) // 128
                    bld.op("v", lambda e, kvb=kvb, oi=oi: e.tensor_add(
                        out=t["Sb"][:, kvb:kvb + 512], in0=t["Sb"][:, kvb:kvb + 512],
                        in1=t["cmask"][:, oi * 512:(oi + 1) * 512]))
            bld.tiny("v", lambda e, ncols=ncols: e.tensor_reduce(
                out=t["stat"][:, 6:7], in_=t["Sb"][:, 0:ncols], axis=AX.X,
                op=ALU.max, negate=True))
            if first:
                bld.tiny("v", lambda e: e.tensor_copy(
                    out=t["stat"][:, 4:5], in_=t["stat"][:, 6:7]))
                bld.op("v", lambda e: e.memset(t["stat"][:, 5:6], 0.0))
                bld.op("v", lambda e: e.memset(t["osb"], 0.0))
                bld.spacer("v")
            else:
                bld.tiny("v", lambda e: e.tensor_copy(out=t["stat"][:, 7:8],
                                                      in_=t["stat"][:, 4:5]))
                bld.tiny("v", lambda e: e.tensor_tensor(
                    out=t["stat"][:, 4:5], in0=t["stat"][:, 4:5],
                    in1=t["stat"][:, 6:7], op=ALU.min))
                mvs = bld.tiny("v", lambda e: e.tensor_sub(
                    out=t["stat"][:, 7:8], in0=t["stat"][:, 4:5], in1=t["stat"][:, 7:8]))
                bld.wait("s", "v", mvs)
                mse = bld.tiny("s", lambda e: e.activation(
                    out=t["stat"][:, 7:8], in_=t["stat"][:, 7:8], func=AF.Exp))
                bld.wait("v", "s", mse)
                bld.tiny("v", lambda e: e.tensor_scalar_mul(
                    out=t["stat"][:, 5:6], in0=t["stat"][:, 5:6],
                    scalar1=t["stat"][:, 7:8]))
                bld.tiny("v", lambda e: e.tensor_scalar_mul(
                    out=t["osb"], in0=t["osb"], scalar1=t["stat"][:, 7:8]))
            mvv = bld.mark("v")
            bld.wait("s", "v", mvv)
            ms = bld.tiny("s", lambda e, ncols=ncols: e.activation(
                out=t["Pb"][:, 0:ncols], in_=t["Sb"][:, 0:ncols], func=AF.Exp,
                bias=t["stat"][:, 4:5], accum_out=t["stat"][:, 6:7]))
            bld.wait("v", "s", ms)
            bld.tiny("v", lambda e: e.tensor_add(
                out=t["stat"][:, 5:6], in0=t["stat"][:, 5:6], in1=t["stat"][:, 6:7]))
            nkb = min((q0 + 128 - k0 + 127) // 128, ncols // 128)
            bld.wait("t", "s", ms)
            for g0 in range(0, nkb, 4):
                gn = min(4, nkb - g0)
                for k in range(gn):
                    bld.op("t", lambda e, k=k, g0=g0: e.transpose(
                        out=P["T"][:, k * 128:(k + 1) * 128],
                        in_=t["Pb"][:, (g0 + k) * 128:(g0 + k + 1) * 128],
                        identity=t["ident"]))
                bld.wait("v", "t")
                bld.op("v", lambda e, gn=gn: e.tensor_copy(
                    out=t["PTs"][:, 0:gn * 128], in_=P["T"][:, 0:gn * 128]))
                bld.wait("t", "v")
                for k in range(gn):
                    bld.op("t", lambda e, k=k, g0=g0, k0=k0, nkb=nkb: e.matmul(
                        out=P["O"][:], lhsT=t["PTs"][:, k * 128:(k + 1) * 128],
                        rhs=t["vall"][:, k0 + (g0 + k) * 128:k0 + (g0 + k + 1) * 128],
                        start=(g0 + k == 0), stop=(g0 + k == nkb - 1)))
            bld.wait("v", "t")
            bld.op("v", lambda e: e.tensor_add(out=t["osb"], in0=t["osb"], in1=P["O"][:]))
            bld.wait("t", "v")
            first = False
        bld.tiny("v", lambda e: e.reciprocal(out=t["stat"][:, 5:6], in_=t["stat"][:, 5:6]))
        mo2 = bld.op("v", lambda e: e.tensor_scalar_mul(
            out=t["osb"], in0=t["osb"], scalar1=t["stat"][:, 5:6]))
        bld.wait("t", "v", mo2)
        bld.op("t", lambda e: e.transpose(out=P["O"][:], in_=t["osb"], identity=t["identf"]))
        bld.wait("v", "t")
        bld.op("v", lambda e: e.tensor_copy(out=t["otb"], in_=P["O"][:]))
        bld.wait("t", "v")
        for ee in range(8):
            bld.wait("g", "v")
            bld.wait("g", "t")
            mrq = bld.dma("g", lambda e, ee=ee, tof=tof, qt=qt: e.dma_start(
                out=t["rwbc"][:, 0:128],
                in_=d["rwTd"][ee:ee + 1, tof + qt * 128:tof + (qt + 1) * 128]
                .partition_broadcast(128)))
            bld.wait("v", "g", mrq)
            bld.op("v", lambda e, ee=ee: e.tensor_mul(
                out=t["awt"][:, ee * 128:(ee + 1) * 128], in0=t["otb"],
                in1=t["rwbc"][:, 0:128]))
        mawt = bld.mark("v")
        bld.wait("t", "v", mawt)
        for ch in range(2):
            pm = P["E2"] if ch == 0 else P["F2"]
            for ee in range(8):
                bld.op("t", lambda e, ee=ee, ch=ch, pm=pm: e.matmul(
                    out=pm[:], lhsT=t["awt"][:, ee * 128:(ee + 1) * 128],
                    rhs=t["pjw"][:, ee * 1024 + ch * 512:ee * 1024 + (ch + 1) * 512],
                    start=(ee == 0), stop=(ee == 7)))
        bld.wait("v", "t")
        bld.op("v", lambda e, qt=qt: e.tensor_add(
            out=t["hs"][:, qt * 1024:qt * 1024 + 512],
            in0=t["hs"][:, qt * 1024:qt * 1024 + 512], in1=P["E2"][:]))
        bld.op("v", lambda e, qt=qt: e.tensor_add(
            out=t["hs"][:, qt * 1024 + 512:(qt + 1) * 1024],
            in0=t["hs"][:, qt * 1024 + 512:(qt + 1) * 1024], in1=P["F2"][:]))
        bld.wait("t", "v")

    if DEBUG:
        bld.wait("sp", "v")
        for tt in range(8):
            bld.dma("sp", lambda e, tt=tt, half=half: e.dma_start(
                out=d["dbg_hs"][half * 8 + tt], in_=t["hs"][:, tt * 1024:(tt + 1) * 1024]))


def moe_half(bld, ctx, p, half, H):
    t, P, d = ctx["t"], ctx["P"], ctx["d"]
    ln_1024, poll_mate, flag_self = H["ln_1024"], H["poll_mate"], H["flag_self"]
    tof = half * HT
    for tt in range(8):
        hsl = t["hs"][:, tt * 1024:(tt + 1) * 1024]
        ln_1024(hsl, t["n1f"], (t["lnw"], t["lnb"]),
                accum_to=t["n1ni"][:, 0:1])
        mv = bld.op("v", lambda e: e.tensor_copy(out=t["awt"][:, 0:1024], in_=t["n1f"]))
        bld.wait("t", "v", mv)
        for cb in range(8):
            bld.op("t", lambda e, cb=cb: e.transpose(
                out=P["T"][:, (cb % 4) * 128:(cb % 4) * 128 + 128],
                in_=t["awt"][:, cb * 128:(cb + 1) * 128], identity=t["ident"]))
            if cb % 4 == 3:
                bld.wait("v", "t")
                g0 = cb - 3
                bld.op("v", lambda e, g0=g0, tt=tt: e.tensor_copy(
                    out=t["nT"].rearrange("p (cb c) -> p cb c", c=1024)
                    [:, g0:g0 + 4, tt * 128:(tt + 1) * 128],
                    in_=P["T"][:, 0:512]))
                bld.wait("t", "v")
        mrw = gating_tt(bld, ctx, tt, t["n1ni"][:, 0:1])
        if DEBUG and half == 0 and tt == 0:
            bld.wait("sp", "v", mrw)
            bld.wait("sp", "s")
            bld.op("v", lambda e: e.tensor_copy(
                out=t["x1a"][:, 768:832], in_=t["simb"]))
            mdv3 = bld.mark("v")
            bld.wait("sp", "v", mdv3)
            bld.dma("sp", lambda e: e.dma_start(
                out=d["dbg_moe"][0][:, 256:264], in_=t["rw8"]))
            bld.dma("sp", lambda e: e.dma_start(
                out=d["dbg_moe"][0][:, 264:272], in_=t["mk8"]))
            bld.dma("sp", lambda e: e.dma_start(
                out=d["dbg_moe"][0][:, 272:280], in_=t["sgb"]))
            bld.dma("sp", lambda e: e.dma_start(
                out=d["dbg_moe"][0][:, 280:288], in_=t["simi"]))
            bld.dma("sp", lambda e: e.dma_start(
                out=d["dbg_moe"][0][:, 288:296], in_=t["lg8"]))
            bld.dma("sp", lambda e: e.dma_start(
                out=d["dbg_moe"][0][:, 296:304], in_=t["stat"][:, 8:16]))
            bld.dma("sp", lambda e: e.dma_start(
                out=d["dbg_moe"][0][:, 304:368], in_=t["x1a"][:, 768:832]))
            mdd3 = bld.mark("sp")
            bld.wait("v", "sp", mdd3)
            bld.wait("t", "sp", mdd3)
            bld.wait("s", "sp", mdd3)
        bld.wait("t", "v", mrw)
        bld.op("t", lambda e: e.transpose(
            out=P["O"][0:8, 0:128], in_=t["rw8"], identity=t["identf"]))
        bld.wait("v", "t")
        bld.op("v", lambda e, tt=tt, half=half: e.tensor_copy(
            out=t["rwTs"][0:8, half * 1024 + tt * 128:half * 1024 + (tt + 1) * 128],
            in_=P["O"][0:8, 0:128]))
        bld.wait("t", "v")
    bld.wait("sp", "v")
    bld.dma("sp", lambda e, half=half: e.dma_start(
        out=d["rwTd"][:, half * 1024:(half + 1) * 1024],
        in_=t["rwTs"][0:8, half * 1024:(half + 1) * 1024]))
    if DEBUG and half == 0:
        bld.dma("sp", lambda e: e.dma_start(
            out=d["dbg_moe"][3][0:8, 512:1536], in_=t["rwTs"][0:8, 0:1024]))

    gmark = {0: None, 1: None}
    mout = bld.mark("t")
    for ee in range(8):
        bld.wait("v", "t")
        for wsl, wdd in (("w1c", "w1"), ("w2c", "w2")):
            for hh in range(2):
                bld.wait("sp", "v")
                mw = bld.dma("sp", lambda e, wdd=wdd, ee=ee, hh=hh: e.dma_start(
                    out=_fstage2(t),
                    in_=d[wdd][ee].rearrange("(a p) c -> p a c", p=128)
                    [:, hh * 4:(hh + 1) * 4, :]))
                bld.wait("v", "sp", mw)
                bld.op("v", lambda e, wsl=wsl, hh=hh: e.tensor_copy(
                    out=t[wsl][:, hh * 4096:(hh + 1) * 4096], in_=_fstage2(t)))
        bld.wait("t", "v")
        for tc in range(2):
            bld.wait("g", "v")
            bld.wait("g", "sp")
            mrq = bld.dma("g", lambda e, ee=ee, tc=tc, tof=tof: e.dma_start(
                out=t["rwbc"][:, 0:512],
                in_=d["rwTd"][ee:ee + 1, tof + tc * 512:tof + (tc + 1) * 512]
                .partition_broadcast(128)))
            for it in range(8):
                pm = P["A"] if it % 2 == 0 else P["B2"]
                par = it % 2
                if gmark[par] is None:
                    bld.wait("t", "s")
                else:
                    bld.wait("t", "s", gmark[par])
                for kt in range(8):
                    bld.op("t", lambda e, it=it, kt=kt, tc=tc, pm=pm: e.matmul(
                        out=pm[:, 0:512],
                        lhsT=t["w1c"][:, kt * 1024 + it * 128:kt * 1024 + (it + 1) * 128],
                        rhs=t["nT"][:, kt * 1024 + tc * 512:kt * 1024 + (tc + 1) * 512],
                        start=(kt == 0), stop=(kt == 7)))
                bld.wait("s", "t")
                bld.wait("s", "v")
                bld.op("s", lambda e, pm=pm: e.activation(
                    out=t["gel"], in_=pm[:, 0:512], func=AF.Gelu))
                gmark[par] = bld.mark("s")
                bld.wait("v", "s")
                if it == 0:
                    bld.wait("v", "g", mrq)
                    bld.wait("v", "t", mout)
                bld.op("v", lambda e, it=it: e.tensor_mul(
                    out=t["iwb"][:, it * 512:(it + 1) * 512], in0=t["gel"],
                    in1=t["rwbc"][:, 0:512]))
                if DEBUG and half == 0 and ee == 0 and tc == 1 and it in (2, 4, 6, 7):
                    mdv9 = bld.mark("v")
                    bld.wait("g", "v", mdv9)
                    col = 1792 if it == 7 else 1280 + (it // 2 - 1) * 256
                    mdg9 = bld.dma("g", lambda e, col=col: e.dma_start(
                        out=d["dbg_moe"][7][:, col:col + 256], in_=t["iwb"][:, 0:256]))
                    bld.wait("v", "g", mdg9)
                    bld.wait("t", "g", mdg9)
                    bld.wait("s", "g", mdg9)
                if DEBUG and half == 0 and ee == 0 and tc == 1 and it == 0:
                    bld.op("v", lambda e: e.tensor_copy(
                        out=t["x1a"][:, 0:256], in_=pm[:, 0:256]))
                    bld.op("v", lambda e: e.tensor_copy(
                        out=t["x1a"][:, 256:512], in_=t["gel"]))
                    mdvx = bld.mark("v")
                    bld.wait("sp", "v", mdvx)
                    bld.dma("sp", lambda e: e.dma_start(
                        out=d["dbg_moe"][7][:, 512:1024], in_=t["x1a"][:, 0:512]))
                    bld.wait("g", "v", mdvx)
                    mdgx = bld.dma("g", lambda e: e.dma_start(
                        out=d["dbg_moe"][7][:, 1024:1280], in_=t["iwb"][:, 0:256]))
                    mddx = bld.mark("sp")
                    bld.wait("v", "sp", mddx)
                    bld.wait("t", "sp", mddx)
                    bld.wait("s", "sp", mddx)
                    bld.wait("v", "g", mdgx)
            if DEBUG and half == 0 and ee == 0 and tc <= 1:
                ro = 4 * tc
                mdv = bld.mark("v")
                bld.wait("sp", "v", mdv)
                bld.wait("sp", "s")
                bld.op("v", lambda e: e.tensor_copy(
                    out=t["x1a"][:, 0:256], in_=t["w1c"][:, 0:256]))
                bld.op("v", lambda e, tc=tc: e.tensor_copy(
                    out=t["x1a"][:, 256:512], in_=t["nT"][:, tc * 256:(tc + 1) * 256]))
                mdv2 = bld.mark("v")
                bld.wait("sp", "v", mdv2)
                bld.dma("sp", lambda e, ro=ro: e.dma_start(
                    out=d["dbg_moe"][ro + 0][:, 0:256], in_=t["gel"]))
                bld.wait("g", "v", mdv2)
                mdg = bld.dma("g", lambda e, ro=ro: e.dma_start(
                    out=d["dbg_moe"][ro + 1], in_=t["iwb"]))
                bld.wait("v", "g", mdg)
                bld.dma("sp", lambda e, ro=ro: e.dma_start(
                    out=d["dbg_moe"][ro + 2][:, 0:256], in_=t["rwbc"][:, 0:256]))
                bld.dma("sp", lambda e, ro=ro: e.dma_start(
                    out=d["dbg_moe"][ro + 3][:, 0:512], in_=t["x1a"][:, 0:512]))
                mdd = bld.mark("sp")
                bld.wait("v", "sp", mdd)
                bld.wait("t", "sp", mdd)
                bld.wait("s", "sp", mdd)
            miw = bld.mark("v")
            bld.wait("t", "v", miw)
            for tt2 in range(4):
                for ch in range(2):
                    pm = P["E2"] if ch == 0 else P["F2"]
                    for it in range(8):
                        bld.op("t", lambda e, it=it, tt2=tt2, ch=ch, pm=pm: e.matmul(
                            out=pm[:],
                            lhsT=t["iwb"][:, it * 512 + tt2 * 128:it * 512 + tt2 * 128 + 128],
                            rhs=t["w2c"][:, it * 1024 + ch * 512:it * 1024 + (ch + 1) * 512],
                            start=(it == 0), stop=(it == 7)))
                bld.wait("v", "t")
                ttg = tc * 4 + tt2
                bld.op("v", lambda e, ttg=ttg: e.tensor_add(
                    out=t["hs"][:, ttg * 1024:ttg * 1024 + 512],
                    in0=t["hs"][:, ttg * 1024:ttg * 1024 + 512], in1=P["E2"][:]))
                bld.op("v", lambda e, ttg=ttg: e.tensor_add(
                    out=t["hs"][:, ttg * 1024 + 512:(ttg + 1) * 1024],
                    in0=t["hs"][:, ttg * 1024 + 512:(ttg + 1) * 1024], in1=P["F2"][:]))
                bld.wait("t", "v")
            mout = bld.mark("t")

    if DEBUG:
        bld.wait("sp", "v")
        for tt in range(8):
            bld.dma("sp", lambda e, tt=tt, half=half: e.dma_start(
                out=d["dbg_hs2"][half * 8 + tt], in_=t["hs"][:, tt * 1024:(tt + 1) * 1024]))
    for tt in range(8):
        bld.wait("sp", "v")
        mx = bld.dma("sp", lambda e, tt=tt, tof=tof: e.dma_start(
            out=t["x1a"], in_=d["x1own"][tof + tt * 128:tof + (tt + 1) * 128, :]))
        bld.wait("v", "sp", mx)
        mo3 = bld.op("v", lambda e, tt=tt: e.tensor_add(
            out=t["x1a"], in0=t["x1a"], in1=t["hs"][:, tt * 1024:(tt + 1) * 1024]))
        bld.wait("sp", "v", mo3)
        bld.dma("sp", lambda e, tt=tt, tof=tof: e.dma_start(
            out=d["out_x2"][tof + tt * 128:tof + (tt + 1) * 128, :], in_=t["x1a"]))


def _fstage2(t):
    return t["fstgA"]
